# version 16
# baseline (speedup 1.0000x reference)
import numpy as np
import os
import contextlib
import concourse.bass as bass
import concourse.mybir as mybir
from concourse.bass import IndirectOffsetOnAxis
from concourse.bass_utils import run_bass_kernel_spmd

F32 = mybir.dt.float32
BF16 = mybir.dt.bfloat16
U32 = mybir.dt.uint32
I32 = mybir.dt.int32
AF = mybir.ActivationFunctionType
ALU = mybir.AluOpType
AX = mybir.AxisListType

NCORES = 8
T = 16384
TC = 2048
D = 1024
NH = 16
DH = 64
E = 8
CAP = 4916
SLOTP = 5120
FF = 4096
NCH = 4
CHT = 512
ECH = 512
NECH = SLOTP // ECH
EPS = 1e-5
BIG = float(2 ** 30)
ZROW = E * SLOTP
GCONST = 0.7978845608028654
GC3 = 0.044715


class Cnt:
    def __init__(self, h):
        self.h = h
        self.n = 0


def build_nc(PH=8, SUB=9):
    nc = bass.Bass("TRN2", target_bir_lowering=False)

    def ein(name, shape):
        return nc.dram_tensor(name, list(shape), F32, kind="ExternalInput").ap()

    x_in = ein("x_sl", (TC, D))
    wqT = ein("wqT", (D, D)); wkT = ein("wkT", (D, D))
    wvT = ein("wvT", (D, D)); woT = ein("woT", (D, D))
    bq_c = ein("bq_c", (D, 1))
    bk_r = ein("bk_r", (1, D)); bv_r = ein("bv_r", (1, D)); bo_r = ein("bo_r", (1, D))
    wg_in = ein("wg", (D, E)); bg_r = ein("bg_r", (1, E))
    w1_in = ein("w1c", (D, FF)); b1_c = ein("b1_c", (FF, 1))
    w2_in = ein("w2c", (FF, D)); b2_r = ein("b2_r", (1, D))
    idn_in = ein("idn", (128, 128))
    su_in = ein("su", (128, 128))
    ones_r = ein("ones_r", (1, 128))
    sel_in = ein("selmat", (256, 32))
    myexp_in = ein("myexp", (128, 1))

    out = nc.dram_tensor("out_sl", [TC, D], F32, kind="ExternalOutput").ap()

    KVIN = nc.dram_tensor("KVIN", [NH * DH, DH], F32, kind="Internal")
    KVOUT = nc.dram_tensor("KVOUT", [2 * NH * DH, DH], F32, kind="Internal")
    LGIN = nc.dram_tensor("LGIN", [TC, E], F32, kind="Internal")
    LGOUT = nc.dram_tensor("LGOUT", [T, E], F32, kind="Internal")
    XNIN = nc.dram_tensor("XNIN", [TC, D], BF16, kind="Internal")
    XNOUT = nc.dram_tensor("XNOUT", [T, D], BF16, kind="Internal")
    X1D = nc.dram_tensor("X1D", [TC, D], F32, kind="Internal")
    RT = nc.dram_tensor("RT", [SLOTP, 1], F32, kind="Internal")
    TROWD = nc.dram_tensor("TROWD", [2048, 1], F32, kind="Internal")
    CROWD = nc.dram_tensor("CROWD", [2048, 1], F32, kind="Internal")
    YEIN = nc.dram_tensor("YEIN", [SLOTP, D], BF16, kind="Internal")
    YEOUT = nc.dram_tensor("YEOUT", [E * SLOTP + 1, D], BF16, kind="Internal")

    st0 = contextlib.ExitStack()
    EE = st0.enter_context

    s_idnf = EE(nc.sbuf_tensor("s_idnf", [128, 128], F32))
    s_idnb = EE(nc.sbuf_tensor("s_idnb", [128, 128], BF16))
    s_sub = EE(nc.sbuf_tensor("s_sub", [128, 128], BF16))
    s_ones1f = EE(nc.sbuf_tensor("s_ones1f", [1, 128], F32))
    s_ones2b = EE(nc.sbuf_tensor("s_ones2b", [2, 128], BF16))
    s_onescol = EE(nc.sbuf_tensor("s_onescol", [128, 1], BF16))
    s_bqc = EE(nc.sbuf_tensor("s_bqc", [128, 8], F32))
    s_negbq = EE(nc.sbuf_tensor("s_negbq", [128, 8], F32))
    s_b1c = EE(nc.sbuf_tensor("s_b1c", [128, 32], F32))
    s_bkL = EE(nc.sbuf_tensor("s_bkL", [2, D], BF16))
    s_bvL = EE(nc.sbuf_tensor("s_bvL", [2, D], BF16))
    s_boL = EE(nc.sbuf_tensor("s_boL", [2, D], BF16))
    s_b2L = EE(nc.sbuf_tensor("s_b2L", [2, D], BF16))
    s_lscr = EE(nc.sbuf_tensor("s_lscr", [1, 512], BF16))
    s_bgrow = EE(nc.sbuf_tensor("s_bgrow", [1, E], F32))
    s_wgt = EE(nc.sbuf_tensor("s_wgt", [128, 64], F32))
    s_myexp = EE(nc.sbuf_tensor("s_myexp", [128, 1], F32))
    s_sel = EE(nc.sbuf_tensor("s_sel", [128, 64], F32))
    s_zf = EE(nc.sbuf_tensor("s_zf", [128, 64], F32))
    s_zb = EE(nc.sbuf_tensor("s_zb", [1, 512], BF16))
    s_cst = EE(nc.sbuf_tensor("s_cst", [128, 2], F32))
    s_io8 = EE(nc.sbuf_tensor("s_io8", [128, 8], F32))
    s_iom8 = EE(nc.sbuf_tensor("s_iom8", [128, 8], F32))
    s_io8b = EE(nc.sbuf_tensor("s_io8b", [128, 8], BF16))
    s_gidxOwnU = EE(nc.sbuf_tensor("s_gidxOwnU", [128, 32], U32))
    s_gOwn = EE(nc.sbuf_tensor("s_gOwn", [128, 32], F32))
    s_rti = EE(nc.sbuf_tensor("s_rti", [128, 40], U32))

    pP = [EE(nc.psum_tensor("pP%d" % i, [128, 512], F32)) for i in range(7)]
    pB = EE(nc.psum_tensor("pB", [128, 512], BF16))

    sIN = Cnt(EE(nc.semaphore("sIN")))
    sROW = Cnt(EE(nc.semaphore("sROW")))
    sROW2 = Cnt(EE(nc.semaphore("sROW2")))
    sZ = Cnt(EE(nc.semaphore("sZ")))
    sX = Cnt(EE(nc.semaphore("sX")))
    sKV = Cnt(EE(nc.semaphore("sKV")))
    sST = Cnt(EE(nc.semaphore("sST")))
    sRT = Cnt(EE(nc.semaphore("sRT")))
    sYE = [Cnt(EE(nc.semaphore("sYE%d" % j))) for j in range(5)]
    sXR = Cnt(EE(nc.semaphore("sXR")))
    sOUT = Cnt(EE(nc.semaphore("sOUT")))
    gK = Cnt(EE(nc.semaphore("gK")))
    gW = Cnt(EE(nc.semaphore("gW")))
    gW1 = Cnt(EE(nc.semaphore("gW1")))
    gSC = Cnt(EE(nc.semaphore("gSC")))
    gSC2 = Cnt(EE(nc.semaphore("gSC2")))
    gXG = [Cnt(EE(nc.semaphore("gXGa"))), Cnt(EE(nc.semaphore("gXGb")))]
    gFG = [Cnt(EE(nc.semaphore("gFG%d" % j))) for j in range(4)]
    tPE = Cnt(EE(nc.semaphore("tPE")))
    vDV = Cnt(EE(nc.semaphore("vDV")))
    aAC = Cnt(EE(nc.semaphore("aAC")))
    cCC = Cnt(EE(nc.semaphore("cCC")))
    gPC = Cnt(EE(nc.semaphore("gPC")))
    ALLC = [sIN, sROW, sROW2, sZ, sX, sKV, sST, sRT] + sYE + [
        sXR, sOUT, gK, gW, gW1, gSC, gSC2, gXG[0], gXG[1]] + gFG + [
        tPE, vDV, aAC, cCC, gPC]

    block = EE(nc.Block())

    def waitall(e, *cnts):
        for c in (cnts or ALLC):
            if c.n > 0:
                e.wait_ge(c.h, c.n)

    s_xt = EE(nc.sbuf_tensor("s_xt", [128, 4 * D], F32))
    s_qf = EE(nc.sbuf_tensor("s_qf", [128, 8 * TC], BF16))
    s_brow = s_qf.ap().bitcast(F32)[0:1, 0:1024]
    @block.sync
    def _(e):
        for tb in range(4):
            e.dma_start(out=s_xt[:, tb * D:(tb + 1) * D],
                        in_=x_in[tb * 128:(tb + 1) * 128, :]).then_inc(sX.h, 16); sX.n += 16
    m_x_ch0 = sX.n

    @block.sync
    def _(e):
        def d(o, i):
            e.dma_start(out=o, in_=i).then_inc(sIN.h, 16); sIN.n += 16
        d(s_idnf[:], idn_in[:])
        for fc in range(8):
            d(s_bqc[:, fc:fc + 1], bq_c[fc * 128:(fc + 1) * 128, :])
        for fc in range(32):
            d(s_b1c[:, fc:fc + 1], b1_c[fc * 128:(fc + 1) * 128, :])
        d(s_bgrow[:], bg_r[:])
        for dc in range(8):
            d(s_wgt[:, dc * 8:(dc + 1) * 8], wg_in[dc * 128:(dc + 1) * 128, :])
        d(s_myexp[:], myexp_in[:])
        for uc in range(2):
            d(s_sel[:, uc * 32:(uc + 1) * 32], sel_in[uc * 128:(uc + 1) * 128, :])
        d(s_ones1f[:], ones_r[:])
    m_sy_pre = sIN.n

    @block.gpsimd
    def _(e):
        def d(o, i):
            e.dma_start(out=o, in_=i).then_inc(gK.h, 16); gK.n += 16
        d(s_idnb[:], idn_in[:])
        d(s_sub[:], su_in[:])
        d(s_ones2b[0:1, :], ones_r[:])
        d(s_ones2b[1:2, :], ones_r[:])
        d(s_onescol[:], ones_r.rearrange("a p -> p a"))
        e.memset(s_zf[:], 0.0).then_inc(gPC.h, 1); gPC.n += 1
        e.memset(s_zb[:], 0.0).then_inc(gPC.h, 1); gPC.n += 1
        e.memset(s_cst[:, 0:1], EPS).then_inc(gPC.h, 1); gPC.n += 1
        e.memset(s_cst[:, 1:2], 0.5).then_inc(gPC.h, 1); gPC.n += 1
    m_gp_consts = gK.n
    m_gpc_consts = gPC.n

    dv_limb0 = vDV.n
    row_ms = lambda i: 16 * (i + 1)
    cp_ms = lambda i, h: 16 * (2 * i + h + 1)
    dv_ms = lambda i, k: dv_limb0 + 3 * i + k
    @block.sync
    def _(e):
        rows = [bk_r, bv_r, bo_r, b2_r]
        dsts = [s_bkL, s_bvL, s_boL, s_b2L]
        e.dma_start(out=s_brow[:], in_=rows[0][:]).then_inc(sROW2.h, 16); sROW2.n += 16
        for i in range(4):
            for h in range(2):
                e.wait_ge(vDV.h, dv_ms(i, 2 + h))
                e.dma_start(out=dsts[i][1:2, h * 512:(h + 1) * 512], in_=s_lscr[:]).then_inc(sROW.h, 16); sROW.n += 16
            if i < 3:
                e.wait_ge(vDV.h, dv_ms(i, 3))
                e.dma_start(out=s_brow[:], in_=rows[i + 1][:]).then_inc(sROW2.h, 16); sROW2.n += 16
    assert sROW.n == 8 * 16 and sROW2.n == 4 * 16
    @block.vector
    def _(e):
        e.wait_ge(sIN.h, m_sy_pre)
        for i, dst in enumerate([s_bkL, s_bvL, s_boL, s_b2L]):
            e.wait_ge(sROW2.h, row_ms(i))
            e.tensor_copy(dst[0:1, :], s_brow[:]).then_inc(vDV.h, 1); vDV.n += 1
            for h in range(2):
                if i + h > 0:
                    prev = cp_ms(i, h - 1) if h == 1 else cp_ms(i - 1, 1)
                    e.wait_ge(sROW.h, prev)
                e.wait_ge(vDV.h, vDV.n)
                e.tensor_tensor(s_lscr[:], s_brow[:, h * 512:(h + 1) * 512],
                                dst[0:1, h * 512:(h + 1) * 512], ALU.subtract).then_inc(vDV.h, 1); vDV.n += 1
        e.tensor_scalar(s_negbq[:], s_bqc[:], -1.0, None, op0=ALU.mult).then_inc(vDV.h, 1); vDV.n += 1
    m_dv_pre = vDV.n

    @block.sync
    def _(e):
        e.wait_ge(gK.h, m_gp_consts)
        e.wait_ge(gPC.h, m_gpc_consts)
        e.dma_start(out=RT.ap().rearrange("(p a) b -> p (a b)", p=128),
                    in_=s_zf[:, 0:40]).then_inc(sZ.h, 16); sZ.n += 16
        e.dma_start(out=YEOUT.ap()[ZROW:ZROW + 1, 0:512], in_=s_zb[:]).then_inc(sZ.h, 16); sZ.n += 16
        e.dma_start(out=YEOUT.ap()[ZROW:ZROW + 1, 512:1024], in_=s_zb[:]).then_inc(sZ.h, 16); sZ.n += 16
    m_sy_zero = sZ.n

    if PH <= 1:
        st0.close()
        return nc

    s_wqT = EE(nc.sbuf_tensor("s_wqT", [128, 8 * D], BF16))
    s_wkT = EE(nc.sbuf_tensor("s_wkT", [128, 8 * D], BF16))
    s_wvT = EE(nc.sbuf_tensor("s_wvT", [128, 8 * D], BF16))
    s_woT = EE(nc.sbuf_tensor("s_woT", [128, 8 * D], BF16))
    s_xnf = EE(nc.sbuf_tensor("s_xnf", [128, 8 * TC], BF16))
    s_k = EE(nc.sbuf_tensor("s_k", [128, 4 * D], BF16))
    s_v = EE(nc.sbuf_tensor("s_v", [128, 4 * D], BF16))
    s_stat = EE(nc.sbuf_tensor("s_stat", [128, 8], F32))
    s_sq = EE(nc.sbuf_tensor("s_sq", [128, D], F32))
    s_kvb = EE(nc.sbuf_tensor("s_kvb", [128, 1024], BF16))
    s_attnf = EE(nc.sbuf_tensor("s_attnf", [128, 8 * CHT], BF16))
    s_x1 = EE(nc.sbuf_tensor("s_x1", [128, D], F32))
    s_kvf = s_x1.ap()[0:64, :]
    s_kv2 = s_attnf.ap().bitcast(F32)[0:64, 0:1024]
    KB32 = s_k.ap().bitcast(F32)
    VB32 = s_v.ap().bitcast(F32)
    s_xn1b = s_v.ap()[:, 0:1024]
    s_xn1f = KB32[:, 0:1024]
    s_xn1T = KB32[:, 1024:2048]
    s_lgt = VB32[:, 512:520]

    @block.gpsimd
    def _(e):
        for dst, src in [(s_wqT, wqT), (s_wkT, wkT), (s_wvT, wvT), (s_woT, woT)]:
            for ec in range(8):
                e.dma_start(out=dst[:, ec * D:(ec + 1) * D],
                            in_=src[ec * 128:(ec + 1) * 128, :]).then_inc(gW.h, 16); gW.n += 16
    m_gp_wattn = gW.n

    for ch in range(NCH):
        if ch == 0:
            m_x = m_x_ch0
        else:
            @block.sync
            def _(e, ch=ch):
                waitall(e, vDV, tPE)
                for tb in range(4):
                    e.dma_start(out=s_xt[:, tb * D:(tb + 1) * D],
                                in_=x_in[ch * CHT + tb * 128:ch * CHT + (tb + 1) * 128, :]).then_inc(sX.h, 16); sX.n += 16
            m_x = sX.n

        @block.scalar
        def _(e):
            e.wait_ge(sX.h, m_x)
            for tb in range(4):
                if tb > 0:
                    e.wait_ge(aAC.h, aAC.n)
                e.activation(s_sq[:], s_xt[:, tb * D:(tb + 1) * D], AF.Square,
                             accum_out=s_stat[:, 2 * tb + 1:2 * tb + 2]).then_inc(aAC.h, 1); aAC.n += 1
        m_sq = aAC.n
        @block.vector
        def _(e):
            e.wait_ge(sX.h, m_x)
            for tb in range(4):
                e.tensor_reduce(s_stat[:, 2 * tb:2 * tb + 1], s_xt[:, tb * D:(tb + 1) * D],
                                axis=AX.X, op=ALU.add).then_inc(vDV.h, 1); vDV.n += 1
            e.wait_ge(aAC.h, m_sq)
            e.wait_ge(vDV.h, vDV.n)
            for tb in range(4):
                m = s_stat[:, 2 * tb:2 * tb + 1]
                sq = s_stat[:, 2 * tb + 1:2 * tb + 2]
                e.tensor_scalar(m, m, 1.0 / D, None, op0=ALU.mult).then_inc(vDV.h, 1); vDV.n += 1
                e.wait_ge(vDV.h, vDV.n)
                e.tensor_tensor(s_sq[:, 0:1], m, m, ALU.mult).then_inc(vDV.h, 1); vDV.n += 1
                e.wait_ge(vDV.h, vDV.n)
                e.tensor_scalar(sq, sq, 1.0 / D, s_sq[:, 0:1], op0=ALU.mult, op1=ALU.subtract).then_inc(vDV.h, 1); vDV.n += 1
                e.wait_ge(vDV.h, vDV.n)
        m_var = vDV.n
        @block.scalar
        def _(e, ch=ch):
            e.wait_ge(vDV.h, m_var)
            if ch == 0:
                e.wait_ge(gPC.h, m_gpc_consts)
            for tb in range(4):
                sq = s_stat[:, 2 * tb + 1:2 * tb + 2]
                e.activation(sq, sq, AF.Sqrt, bias=s_cst[:, 0:1]).then_inc(aAC.h, 1); aAC.n += 1
        m_sqrt = aAC.n
        @block.vector
        def _(e):
            e.wait_ge(aAC.h, m_sqrt)
            for tb in range(4):
                m = s_stat[:, 2 * tb:2 * tb + 1]
                sq = s_stat[:, 2 * tb + 1:2 * tb + 2]
                e.reciprocal(sq, sq).then_inc(vDV.h, 1); vDV.n += 1
                e.wait_ge(vDV.h, vDV.n)
                e.tensor_tensor(m, m, sq, ALU.mult).then_inc(vDV.h, 1); vDV.n += 1
                e.wait_ge(vDV.h, vDV.n)
                e.tensor_scalar(s_k[:, tb * D:(tb + 1) * D], s_xt[:, tb * D:(tb + 1) * D],
                                sq, m, op0=ALU.mult, op1=ALU.subtract).then_inc(vDV.h, 1); vDV.n += 1
        m_xn = vDV.n

        if SUB < 2:
            continue
        tr_pe0 = tPE.n
        tr_dv0 = vDV.n
        trslot1 = [pB[:, 0:128], pP[6].ap().bitcast(BF16)[:, 0:128]]
        @block.tensor
        def _(e, ch=ch):
            e.wait_ge(vDV.h, m_xn)
            e.wait_ge(gK.h, m_gp_consts)
            for i in range(32):
                tb, ec = i // 8, i % 8
                if i >= 2:
                    e.wait_ge(vDV.h, tr_dv0 + i - 1)
                e.transpose(trslot1[i % 2],
                            s_k[:, tb * D + ec * 128: tb * D + (ec + 1) * 128],
                            s_idnb[:]).then_inc(tPE.h, 1); tPE.n += 1
        assert tPE.n == tr_pe0 + 32
        @block.vector
        def _(e, ch=ch):
            for i in range(32):
                tb, ec = i // 8, i % 8
                e.wait_ge(tPE.h, tr_pe0 + i + 1)
                e.tensor_copy(s_xnf[:, ec * TC + ch * CHT + tb * 128: ec * TC + ch * CHT + (tb + 1) * 128],
                              trslot1[i % 2]).then_inc(vDV.h, 1); vDV.n += 1
        assert vDV.n == tr_dv0 + 32
        m_xnf = vDV.n

        if SUB < 3:
            continue
        kv_pe0 = tPE.n
        kv_dv0 = vDV.n
        kv_ac0 = aAC.n
        k_u = lambda idx: kv_dv0 + 2 * idx + 1
        k_ev = lambda idx: kv_dv0 + 2 * idx + 2
        v_ev = lambda idx: kv_dv0 + 16 + idx + 1
        k_ac = lambda idx: kv_ac0 + 2 * idx + 2
        k_pe = lambda idx: kv_pe0 + idx + 1
        v_pe = lambda idx: kv_pe0 + 8 + idx + 1
        @block.tensor
        def _(e, ch=ch):
            e.wait_ge(vDV.h, m_xnf)
            e.wait_ge(gW.h, m_gp_wattn)
            e.wait_ge(vDV.h, m_dv_pre)
            e.wait_ge(sROW.h, 128)
            for which, W, bias, banks, evf in [("k", s_wkT, s_bkL, (pP[0], pP[1]), k_ev),
                                               ("v", s_wvT, s_bvL, (pP[2], pP[3]), v_ev)]:
                for idx in range(8):
                    tb, hf = idx // 2, idx % 2
                    pb = banks[idx % 2]
                    if idx >= 2:
                        e.wait_ge(vDV.h, evf(idx - 2))
                    ins = None
                    for ec in range(8):
                        e.matmul(pb[:, 0:512],
                                 s_xnf[:, ec * TC + ch * CHT + tb * 128: ec * TC + ch * CHT + (tb + 1) * 128],
                                 W[:, ec * D + hf * 512: ec * D + (hf + 1) * 512],
                                 start=(ec == 0), stop=False)
                    e.matmul(pb[:, 0:512], s_ones2b[:, 0:128], bias[:, hf * 512:(hf + 1) * 512],
                             start=False, stop=True).then_inc(tPE.h, 1); tPE.n += 1
        assert tPE.n == kv_pe0 + 16
        @block.scalar
        def _(e, ch=ch):
            for idx in range(8):
                e.wait_ge(vDV.h, k_u(idx))
                if idx > 0:
                    e.wait_ge(aAC.h, aAC.n)
                    e.wait_ge(vDV.h, k_ev(idx - 1))
                e.activation(s_sq[:, 512:1024], s_sq[:, 0:512], AF.Relu, scale=-1.0).then_inc(aAC.h, 1); aAC.n += 1
                e.wait_ge(aAC.h, aAC.n)
                e.activation(s_sq[:, 512:1024], s_sq[:, 512:1024], AF.Exp, scale=-1.0).then_inc(aAC.h, 1); aAC.n += 1
        assert aAC.n == kv_ac0 + 16
        @block.vector
        def _(e, ch=ch):
            for idx in range(8):
                tb, hf = idx // 2, idx % 2
                pb = (pP[0], pP[1])[idx % 2]
                e.wait_ge(tPE.h, k_pe(idx))
                if idx > 0:
                    e.wait_ge(aAC.h, k_ac(idx - 1))
                e.wait_ge(vDV.h, vDV.n)
                e.tensor_copy(s_sq[:, 0:512], pb[:, 0:512]).then_inc(vDV.h, 1); vDV.n += 1
                e.wait_ge(aAC.h, k_ac(idx))
                e.wait_ge(vDV.h, vDV.n)
                e.scalar_tensor_tensor(s_k[:, tb * D + hf * 512: tb * D + (hf + 1) * 512],
                                       s_sq[:, 0:512], 0.0, s_sq[:, 512:1024],
                                       op0=ALU.max, op1=ALU.add).then_inc(vDV.h, 1); vDV.n += 1
            for idx in range(8):
                tb, hf = idx // 2, idx % 2
                pb = (pP[2], pP[3])[idx % 2]
                e.wait_ge(tPE.h, v_pe(idx))
                e.tensor_copy(s_v[:, tb * D + hf * 512: tb * D + (hf + 1) * 512], pb[:, 0:512]).then_inc(vDV.h, 1); vDV.n += 1
        assert vDV.n == kv_dv0 + 24
        m_kvev = vDV.n

        if SUB < 4:
            continue
        @block.tensor
        def _(e, ch=ch):
            e.wait_ge(vDV.h, m_kvev)
            for tb in range(4):
                for h in range(NH):
                    bank = pP[4] if h < 8 else pP[5]
                    col = (h % 8) * 64
                    ins = e.matmul(bank[0:64, col:col + 64],
                                   s_k[:, tb * D + h * DH: tb * D + (h + 1) * DH],
                                   s_v[:, tb * D + h * DH: tb * D + (h + 1) * DH],
                                   start=(ch == 0 and tb == 0 and (h % 8) == 0),
                                   stop=(ch == NCH - 1 and tb == 3 and (h % 8) == 7),
                                   skip_group_check=True)
            ins.then_inc(tPE.h, 1); tPE.n += 1
    m_kv_pe = tPE.n

    if SUB < 4:
        st0.close()
        return nc

    @block.vector
    def _(e):
        e.wait_ge(tPE.h, m_kv_pe)
        e.tensor_copy(s_kvf[:, 0:512], pP[4][0:64, :]).then_inc(vDV.h, 1); vDV.n += 1
        e.tensor_copy(s_kvf[:, 512:1024], pP[5][0:64, :]).then_inc(vDV.h, 1); vDV.n += 1
    m_kvf = vDV.n

    if PH <= 2:
        st0.close()
        return nc
    @block.sync
    def _(e):
        e.wait_ge(vDV.h, m_kvf)
        e.dma_start(out=KVIN.ap().rearrange("(h d) e -> d h e", h=16),
                    in_=s_kvf.rearrange("d (h e) -> d h e", e=64)).then_inc(sKV.h, 16); sKV.n += 16
    m_kvin = sKV.n
    @block.gpsimd
    def _(e):
        e.wait_ge(sKV.h, m_kvin)
        e.collective_compute("AllGather", ALU.bypass,
                             replica_groups=[[0, 1], [2, 3], [4, 5], [6, 7]],
                             ins=[KVIN.ap()], outs=[KVOUT.ap()]).then_inc(cCC.h, 1); cCC.n += 1
    m_cc_kv = cCC.n

    q_pe0 = tPE.n
    q_dv0 = vDV.n
    q_ac0 = aAC.n
    q_pe = lambda idx: q_pe0 + idx + 1
    q_u = lambda idx: q_dv0 + 2 * idx + 1
    q_ev = lambda idx: q_dv0 + 2 * idx + 2
    q_ac = lambda idx: q_ac0 + 2 * idx + 2
    q_dv_base0 = vDV.n
    @block.tensor
    def _(e):
        e.wait_ge(vDV.h, q_dv_base0)
        for idx in range(32):
            ch, fc = idx // 8, idx % 8
            if idx >= 2:
                e.wait_ge(vDV.h, q_ev(idx - 2))
                e.wait_ge(aAC.h, q_ac(idx - 2))
            for ec in range(8):
                ins = e.matmul((pP[6], pP[3])[idx % 2][:, 0:512],
                               s_wqT[:, ec * D + fc * 128: ec * D + (fc + 1) * 128],
                               s_xnf[:, ec * TC + ch * CHT: ec * TC + (ch + 1) * CHT],
                               start=(ec == 0), stop=(ec == 7))
            ins.then_inc(tPE.h, 1); tPE.n += 1
    assert tPE.n == q_pe0 + 32
    @block.scalar
    def _(e):
        for idx in range(32):
            e.wait_ge(vDV.h, q_u(idx))
            if idx > 0:
                e.wait_ge(aAC.h, aAC.n)
                e.wait_ge(vDV.h, q_ev(idx - 1))
            e.activation(s_sq[:, 512:1024], s_sq[:, 0:512], AF.Relu, scale=-1.0).then_inc(aAC.h, 1); aAC.n += 1
            e.wait_ge(aAC.h, aAC.n)
            e.activation(s_sq[:, 512:1024], s_sq[:, 512:1024], AF.Exp, scale=-1.0).then_inc(aAC.h, 1); aAC.n += 1
    assert aAC.n == q_ac0 + 64
    @block.vector
    def _(e):
        for idx in range(32):
            ch, fc = idx // 8, idx % 8
            e.wait_ge(tPE.h, q_pe(idx))
            if idx == 0:
                e.wait_ge(sROW2.h, 64)
            if idx > 0:
                e.wait_ge(aAC.h, q_ac(idx - 1))
            e.wait_ge(vDV.h, vDV.n)
            e.tensor_scalar(s_sq[:, 0:512], (pP[6], pP[3])[idx % 2][:, 0:512], s_bqc[:, fc:fc + 1], None,
                            op0=ALU.add).then_inc(vDV.h, 1); vDV.n += 1
            e.wait_ge(aAC.h, q_ac(idx))
            e.wait_ge(vDV.h, vDV.n)
            e.scalar_tensor_tensor(s_qf[:, fc * TC + ch * CHT: fc * TC + (ch + 1) * CHT],
                                   s_sq[:, 0:512], 0.0, s_sq[:, 512:1024],
                                   op0=ALU.max, op1=ALU.add).then_inc(vDV.h, 1); vDV.n += 1
    assert vDV.n == q_dv0 + 64
    m_q = vDV.n

    m_q_pe = q_pe0 + 32
    XTB = s_xt.ap().bitcast(BF16)
    SQB = s_sq.ap().bitcast(BF16)
    X1B = s_x1.ap().bitcast(BF16)

    def w2v(fc):
        if fc < 8:
            return s_wkT[:, fc * 1024:(fc + 1) * 1024]
        if fc < 16:
            return s_wvT[:, (fc - 8) * 1024:(fc - 7) * 1024]
        if fc < 24:
            return s_wqT[:, (fc - 16) * 1024:(fc - 15) * 1024]
        return s_qf[:, (fc - 24) * 1024:(fc - 23) * 1024]

    def w1blk(dc, a, b):
        if dc in (0, 1):
            return s_qf[:, 8192 + dc * FF + a:8192 + dc * FF + b]
        if dc in (2, 3):
            return XTB[:, (dc - 2) * FF + a:(dc - 2) * FF + b]
        if dc == 4:
            return s_attnf[:, a:b]
        if dc == 5:
            if b <= 2048:
                return SQB[:, a:b]
            return X1B[:, a - 2048:b - 2048]
        return s_woT[:, (dc - 6) * FF + a:(dc - 6) * FF + b]

    @block.sync
    def _(e):
        e.wait_ge(cCC.h, m_cc_kv)
        e.dma_start(out=s_kvf.rearrange("d (h e) -> d h e", e=64),
                    in_=KVOUT.ap()[0:1024, :].rearrange("(h d) e -> d h e", h=16)).then_inc(sKV.h, 16); sKV.n += 16
        e.dma_start(out=s_kv2.rearrange("d (h e) -> d h e", e=64),
                    in_=KVOUT.ap()[1024:2048, :].rearrange("(h d) e -> d h e", h=16)).then_inc(sKV.h, 16); sKV.n += 16
    m_kv2 = sKV.n
    @block.vector
    def _(e):
        e.wait_ge(sKV.h, m_kv2)
        e.tensor_tensor(s_kvb[0:64, :], s_kvf[:], s_kv2[:], ALU.add).then_inc(vDV.h, 1); vDV.n += 1
        e.wait_ge(vDV.h, vDV.n)
        e.tensor_copy(s_kvb[64:128, :], s_kvb[0:64, :]).then_inc(vDV.h, 1); vDV.n += 1
    m_kvb = vDV.n

    if PH <= 3:
        st0.close()
        return nc

    prev_lgv = prev_x1 = prev_sq2 = 0
    for ch in range(NCH):
        @block.sync
        def _(e, ch=ch, prev_x1=prev_x1):
            if ch == 0:
                waitall(e, vDV, tPE, aAC)
            else:
                e.wait_ge(vDV.h, prev_x1)
            for tb in range(4):
                e.dma_start(out=s_xt[:, tb * D:(tb + 1) * D],
                            in_=x_in[ch * CHT + tb * 128:ch * CHT + (tb + 1) * 128, :]).then_inc(sX.h, 16); sX.n += 16
        m_x2 = sX.n

        at_pe0 = tPE.n
        at_dv0 = vDV.n
        @block.tensor
        def _(e, ch=ch, prev_lgv=prev_lgv):
            e.wait_ge(vDV.h, max(m_kvb, m_q))
            if ch > 0:
                e.wait_ge(vDV.h, prev_lgv)
            e.wait_ge(sX.h, m_x2)
            for hp in range(8):
                pb = pP[hp % 2]
                if hp >= 2:
                    e.wait_ge(vDV.h, at_dv0 + hp - 1)
                for sub in range(2):
                    h = 2 * hp + sub
                    hb = (h % 2) * 64
                    ins = e.matmul(pb[hb:hb + 64, 0:512],
                                   s_kvb[hb:hb + 64, h * 64:(h + 1) * 64],
                                   s_qf[hb:hb + 64, (h // 2) * TC + ch * CHT: (h // 2) * TC + (ch + 1) * CHT],
                                   start=True, stop=True, skip_group_check=True)
                ins.then_inc(tPE.h, 1); tPE.n += 1
        assert tPE.n == at_pe0 + 8
        @block.vector
        def _(e, ch=ch):
            for hp in range(8):
                pb = pP[hp % 2]
                e.wait_ge(tPE.h, at_pe0 + hp + 1)
                e.tensor_copy(s_attnf[:, hp * 512:(hp + 1) * 512], pb[:, 0:512]).then_inc(vDV.h, 1); vDV.n += 1
        assert vDV.n == at_dv0 + 8
        m_attnf = vDV.n

        MO = [0] * 4; MX1 = [0] * 4; MSQ2 = [0] * 4; MXN1 = [0] * 4
        MTRE = [0] * 4; MLGST = [0] * 4

        def obank(tb, hf):
            return pP[2 + 2 * (tb % 2) + hf]

        def emit_o(tb, ch=ch):
            @block.tensor
            def _(e, ch=ch, tb=tb):
                e.wait_ge(vDV.h, m_attnf)
                if tb == 0:
                    e.wait_ge(sROW.h, 128)
                if tb >= 2:
                    e.wait_ge(vDV.h, MX1[tb - 2])
                for hf in range(2):
                    pb = obank(tb, hf)
                    for fc in range(8):
                        e.matmul(pb[:, 0:512],
                                 s_attnf[:, fc * 512 + tb * 128: fc * 512 + (tb + 1) * 128],
                                 s_woT[:, fc * D + hf * 512: fc * D + (hf + 1) * 512],
                                 start=(fc == 0), stop=False)
                    ins = e.matmul(pb[:, 0:512], s_ones2b[:, 0:128], s_boL[:, hf * 512:(hf + 1) * 512],
                                   start=False, stop=True)
                    ins.then_inc(tPE.h, 1); tPE.n += 1
            MO[tb] = tPE.n

        emit_o(0)
        emit_o(1)
        for tb in range(4):
            m_o = MO[tb]
            @block.vector
            def _(e, ch=ch, tb=tb, m_o=m_o):
                e.wait_ge(tPE.h, m_o)
                e.wait_ge(sST.h, sST.n)
                if tb >= 1:
                    e.wait_ge(aAC.h, MSQ2[tb - 1])
                for hf in range(2):
                    e.tensor_tensor(s_x1[:, hf * 512:(hf + 1) * 512], obank(tb, hf)[:, 0:512],
                                    s_xt[:, tb * D + hf * 512: tb * D + (hf + 1) * 512], ALU.add).then_inc(vDV.h, 1); vDV.n += 1
                e.wait_ge(vDV.h, vDV.n)
                e.tensor_reduce(s_stat[:, 0:1], s_x1[:], axis=AX.X, op=ALU.add).then_inc(vDV.h, 1); vDV.n += 1
            m_x1 = vDV.n; MX1[tb] = m_x1
            @block.scalar
            def _(e, ch=ch, tb=tb, m_x1=m_x1):
                e.wait_ge(vDV.h, m_x1)
                e.wait_ge(aAC.h, aAC.n)
                e.activation(s_sq[:], s_x1[:], AF.Square, accum_out=s_stat[:, 1:2]).then_inc(aAC.h, 1); aAC.n += 1
            m_sq2 = aAC.n; MSQ2[tb] = m_sq2
            @block.vector
            def _(e, ch=ch, tb=tb, m_sq2=m_sq2):
                e.wait_ge(aAC.h, m_sq2)
                e.tensor_scalar(s_stat[:, 0:1], s_stat[:, 0:1], 1.0 / D, None, op0=ALU.mult).then_inc(vDV.h, 1); vDV.n += 1
                e.wait_ge(vDV.h, vDV.n)
                e.tensor_tensor(s_stat[:, 4:5], s_stat[:, 0:1], s_stat[:, 0:1], ALU.mult).then_inc(vDV.h, 1); vDV.n += 1
                e.wait_ge(vDV.h, vDV.n)
                e.tensor_scalar(s_stat[:, 2:3], s_stat[:, 1:2], 1.0 / D, s_stat[:, 4:5],
                                op0=ALU.mult, op1=ALU.subtract).then_inc(vDV.h, 1); vDV.n += 1
            m_var2 = vDV.n
            @block.scalar
            def _(e, ch=ch, tb=tb, m_var2=m_var2):
                e.wait_ge(vDV.h, m_var2)
                e.activation(s_stat[:, 3:4], s_stat[:, 2:3], AF.Sqrt, bias=s_cst[:, 0:1]).then_inc(aAC.h, 1); aAC.n += 1
            m_sqrt2 = aAC.n
            @block.vector
            def _(e, ch=ch, tb=tb, m_sqrt2=m_sqrt2):
                e.wait_ge(aAC.h, m_sqrt2)
                if tb >= 1:
                    e.wait_ge(tPE.h, MTRE[tb - 1])
                e.reciprocal(s_stat[:, 3:4], s_stat[:, 3:4]).then_inc(vDV.h, 1); vDV.n += 1
                e.wait_ge(vDV.h, vDV.n)
                e.tensor_tensor(s_stat[:, 5:6], s_stat[:, 0:1], s_stat[:, 3:4], ALU.mult).then_inc(vDV.h, 1); vDV.n += 1
                e.wait_ge(vDV.h, vDV.n)
                e.tensor_scalar(s_xn1b, s_x1[:], s_stat[:, 3:4], s_stat[:, 5:6],
                                op0=ALU.mult, op1=ALU.subtract).then_inc(vDV.h, 1); vDV.n += 1
                e.tensor_scalar(s_xn1f, s_x1[:], s_stat[:, 3:4], s_stat[:, 5:6],
                                op0=ALU.mult, op1=ALU.subtract).then_inc(vDV.h, 1); vDV.n += 1
            m_xn1 = vDV.n; MXN1[tb] = m_xn1
            @block.sync
            def _(e, ch=ch, tb=tb, m_x1=m_x1, m_xn1=m_xn1):
                rows = slice(ch * CHT + tb * 128, ch * CHT + (tb + 1) * 128)
                e.wait_ge(vDV.h, m_x1)
                e.dma_start(out=X1D.ap()[rows, :], in_=s_x1[:]).then_inc(sST.h, 16); sST.n += 16
                e.wait_ge(vDV.h, m_xn1)
                e.dma_start(out=XNIN.ap()[rows, :], in_=s_xn1b).then_inc(sST.h, 16); sST.n += 16

            tr2_pe0 = tPE.n
            tr2_dv0 = vDV.n
            @block.tensor
            def _(e, ch=ch, tb=tb, m_xn1=m_xn1, tr2_dv0=tr2_dv0):
                e.wait_ge(vDV.h, m_xn1)
                for dc in range(8):
                    if dc >= 2:
                        e.wait_ge(vDV.h, tr2_dv0 + dc - 1)
                    e.transpose(pP[dc % 2][:, 0:128],
                                s_xn1f[:, dc * 128:(dc + 1) * 128], s_idnf[:]).then_inc(tPE.h, 1); tPE.n += 1
            assert tPE.n == tr2_pe0 + 8
            MTRE[tb] = tPE.n
            @block.vector
            def _(e, ch=ch, tb=tb, tr2_pe0=tr2_pe0):
                for dc in range(8):
                    e.wait_ge(tPE.h, tr2_pe0 + dc + 1)
                    e.tensor_copy(s_xn1T[:, dc * 128:(dc + 1) * 128],
                                  pP[dc % 2][:, 0:128]).then_inc(vDV.h, 1); vDV.n += 1
            assert vDV.n == tr2_dv0 + 8
            m_xn1T = vDV.n
            @block.tensor
            def _(e, ch=ch, tb=tb, m_xn1T=m_xn1T):
                e.wait_ge(vDV.h, m_xn1T)
                for dc in range(8):
                    e.matmul(pP[1][:, 256:264],
                             s_xn1T[:, dc * 128:(dc + 1) * 128],
                             s_wgt[:, dc * 8:(dc + 1) * 8],
                             start=(dc == 0), stop=False)
                e.matmul(pP[1][:, 256:264], s_ones1f[:], s_bgrow[:], start=False, stop=True).then_inc(tPE.h, 1); tPE.n += 1
            m_lgp = tPE.n
            if tb + 2 < 4:
                emit_o(tb + 2)
            @block.vector
            def _(e, ch=ch, tb=tb, m_lgp=m_lgp):
                e.wait_ge(tPE.h, m_lgp)
                if tb >= 1:
                    e.wait_ge(sST.h, MLGST[tb - 1])
                e.tensor_copy(s_lgt, pP[1][:, 256:264]).then_inc(vDV.h, 1); vDV.n += 1
            m_lgv = vDV.n
            @block.sync
            def _(e, ch=ch, tb=tb, m_lgv=m_lgv):
                e.wait_ge(vDV.h, m_lgv)
                rows = slice(ch * CHT + tb * 128, ch * CHT + (tb + 1) * 128)
                e.dma_start(out=LGIN.ap()[rows, :], in_=s_lgt).then_inc(sST.h, 16); sST.n += 16
            MLGST[tb] = sST.n
        m_o = MO[3]; m_x1 = MX1[3]; m_sq2 = MSQ2[3]; m_xn1 = MXN1[3]
        prev_lgv = m_lgv
        prev_x1 = MX1[3]; prev_sq2 = MSQ2[3]

    m_attn_sy = sST.n

    if PH <= 4:
        st0.close()
        return nc

    @block.gpsimd
    def _(e):
        e.wait_ge(tPE.h, m_kv_pe)
        for fc in range(16):
            e.dma_start(out=w2v(fc),
                        in_=w2_in[fc * 128:(fc + 1) * 128, :]).then_inc(gW.h, 16); gW.n += 16
        e.wait_ge(tPE.h, m_q_pe)
        for fc in range(16, 24):
            e.dma_start(out=w2v(fc),
                        in_=w2_in[fc * 128:(fc + 1) * 128, :]).then_inc(gW.h, 16); gW.n += 16
        e.wait_ge(tPE.h, at_pe0 + 8)
        for fc in range(24, 32):
            e.dma_start(out=w2v(fc),
                        in_=w2_in[fc * 128:(fc + 1) * 128, :]).then_inc(gW.h, 16); gW.n += 16
        for dc in (0, 1):
            e.dma_start(out=w1blk(dc, 0, FF),
                        in_=w1_in[dc * 128:(dc + 1) * 128, :]).then_inc(gW1.h, 16); gW1.n += 16
        e.wait_ge(tPE.h, m_o)
        for dc in (4, 6, 7):
            e.dma_start(out=w1blk(dc, 0, FF),
                        in_=w1_in[dc * 128:(dc + 1) * 128, :]).then_inc(gW1.h, 16); gW1.n += 16
        e.wait_ge(vDV.h, m_x1)
        for dc in (2, 3):
            e.dma_start(out=w1blk(dc, 0, FF),
                        in_=w1_in[dc * 128:(dc + 1) * 128, :]).then_inc(gW1.h, 16); gW1.n += 16
        e.wait_ge(aAC.h, m_sq2)
        e.dma_start(out=w1blk(5, 0, 2048),
                    in_=w1_in[5 * 128:6 * 128, 0:2048]).then_inc(gW1.h, 16); gW1.n += 16
        e.wait_ge(vDV.h, m_xn1)
        e.wait_ge(sST.h, m_attn_sy)
        e.dma_start(out=w1blk(5, 2048, FF),
                    in_=w1_in[5 * 128:6 * 128, 2048:4096]).then_inc(gW1.h, 16); gW1.n += 16
    m_gp_w1 = gW1.n
    m_gp_w2 = gW.n

    @block.gpsimd
    def _(e):
        e.wait_ge(sST.h, m_attn_sy)
        e.collective_compute("AllGather", ALU.bypass, replica_groups=[list(range(8))],
                             ins=[LGIN.ap()], outs=[LGOUT.ap()]).then_inc(cCC.h, 1); cCC.n += 1
        e.collective_compute("AllGather", ALU.bypass, replica_groups=[list(range(8))],
                             ins=[XNIN.ap()], outs=[XNOUT.ap()]).then_inc(cCC.h, 1); cCC.n += 1
    m_cc_lg = cCC.n - 1
    m_cc_xn = cCC.n

    BAR = dict(t=tPE.n, v=vDV.n, a=aAC.n, s=sST.n)

    s_xtm = EE(nc.sbuf_tensor("s_xtm", [128, 2 * D], BF16))
    s_xfm = EE(nc.sbuf_tensor("s_xfm", [128, 8 * ECH], BF16))
    s_hu = EE(nc.sbuf_tensor("s_hu", [128, 2 * 512], F32))

    X32 = s_xnf.ap().bitcast(F32)
    s_L = X32[:, 0:1024]
    s_r1 = X32[:, 1024:2048]
    s_r2 = X32[:, 2048:3072]
    s_small = X32[:, 3072:4096]
    s_pos8 = X32[:, 4096:6144]
    s_tokt = X32[:, 6144:6400]
    s_idxTf = X32[:, 6400:6656]
    s_gT = X32[:, 6656:6912]
    s_posE = X32[:, 6912:7168]
    s_keep = X32[:, 7168:7424]
    s_ra = X32[:, 7424:7680]
    s_rb = X32[:, 7680:7936]
    s_rc = X32[:, 7936:8192]
    s_ioint = s_xnf.ap().bitcast(I32)[:, 4096:6144]
    s_m = s_xfm.ap()[:, 0:2048]
    s_idxT = s_xfm.ap()[:, 2048:2304]
    VU = s_v.ap().bitcast(U32)
    s_ru = VU[:, 0:256]
    s_destu = VU[:, 256:512]
    s_t512 = VB32[0:1, 512:1024]
    s_TTf = VB32[:, 1024:1040]
    s_carry = VB32[0:1, 1040:1048]
    s_Cf = VB32[:, 1048:1064]
    s_rtl = VB32[:, 1064:1104]
    s_TT = s_v.ap()[:, 2208:2224]

    s_hV = s_xnf.ap()
    s_yout = s_k.ap()
    s_ga0 = s_xnf.ap()[:, 0:4096]
    s_ga1 = s_xnf.ap()[:, 4096:8192]
    s_finS = [X32[:, 4096:5120], X32[:, 5120:6144]]
    s_x1r = X32[:, 6144:8192]

    @block.gpsimd
    def _(e):
        e.wait_ge(tPE.h, BAR['t']); e.wait_ge(vDV.h, BAR['v'])
        e.wait_ge(aAC.h, BAR['a']); e.wait_ge(sST.h, BAR['s'])
        e.iota(s_ioint[:, 0:8], [[1, 8]], base=-8, channel_multiplier=0).then_inc(gPC.h, 1); gPC.n += 1
        e.wait_ge(gPC.h, gPC.n)
        e.tensor_copy(s_iom8[:], s_ioint[:, 0:8]).then_inc(gPC.h, 1); gPC.n += 1
        e.wait_ge(gPC.h, gPC.n)
        e.tensor_scalar(s_io8[:], s_iom8[:], 8.0, None, op0=ALU.add).then_inc(gPC.h, 1); gPC.n += 1
        e.wait_ge(gPC.h, gPC.n)
        e.tensor_copy(s_io8b[:], s_io8[:]).then_inc(gPC.h, 1); gPC.n += 1
        e.wait_ge(gPC.h, gPC.n)
        e.iota(s_ioint[:, 0:256], [[0, 2], [128, 128]], base=0, channel_multiplier=1).then_inc(gPC.h, 1); gPC.n += 1
        e.wait_ge(gPC.h, gPC.n)
        e.tensor_copy(s_tokt[:], s_ioint[:, 0:256]).then_inc(gPC.h, 1); gPC.n += 1
    m_gpc_route = gPC.n

    @block.sync
    def _(e):
        e.wait_ge(cCC.h, m_cc_lg)
        e.wait_ge(tPE.h, BAR['t']); e.wait_ge(vDV.h, BAR['v'])
        e.dma_start(out=s_L[:], in_=LGOUT.ap().rearrange("(p j) c -> p (j c)", p=128)).then_inc(sRT.h, 16); sRT.n += 16
    m_L = sRT.n

    CW = 128
    def sm(i):
        return s_small[:, i * CW:(i + 1) * CW]
    L3 = s_L.rearrange("p (j c) -> p j c", c=8)
    R13 = s_r1.rearrange("p (j c) -> p j c", c=8)
    R23 = s_r2.rearrange("p (j c) -> p j c", c=8)

    @block.vector
    def _(e):
        e.wait_ge(sRT.h, m_L)
        e.wait_ge(gPC.h, m_gpc_route)
        def step(ins):
            ins.then_inc(vDV.h, 1); vDV.n += 1
            e.wait_ge(vDV.h, vDV.n)
        step(e.tensor_reduce(sm(0), L3, axis=AX.X, op=ALU.max))
        step(e.tensor_tensor(R13, L3, sm(0).unsqueeze(2).broadcast_to((128, 128, 8)), ALU.is_equal))
        step(e.tensor_tensor(R23, R13, s_iom8.ap().unsqueeze(1).broadcast_to((128, 128, 8)), ALU.mult))
        step(e.tensor_reduce(sm(1), R23, axis=AX.X, op=ALU.min))
        step(e.tensor_scalar(sm(1), sm(1), 8.0, None, op0=ALU.add))
        step(e.tensor_tensor(R13, s_io8.ap().unsqueeze(1).broadcast_to((128, 128, 8)), sm(1).unsqueeze(2).broadcast_to((128, 128, 8)), ALU.is_equal))
        step(e.tensor_scalar(s_r1[:], s_r1[:], -1e30, None, op0=ALU.mult))
        step(e.tensor_tensor(s_r1[:], s_r1[:], s_L[:], ALU.add))
        step(e.tensor_reduce(sm(2), R13, axis=AX.X, op=ALU.max))
        step(e.tensor_tensor(R13, R13, sm(2).unsqueeze(2).broadcast_to((128, 128, 8)), ALU.is_equal))
        step(e.tensor_tensor(R23, R13, s_iom8.ap().unsqueeze(1).broadcast_to((128, 128, 8)), ALU.mult))
        step(e.tensor_reduce(sm(3), R23, axis=AX.X, op=ALU.min))
        step(e.tensor_scalar(sm(3), sm(3), 8.0, None, op0=ALU.add))
        step(e.tensor_tensor(sm(4), sm(0), sm(2), ALU.subtract))
    m_rt_a = vDV.n

    @block.scalar
    def _(e):
        e.wait_ge(vDV.h, m_rt_a)
        e.activation(sm(5), sm(4), AF.Sigmoid).then_inc(aAC.h, 1); aAC.n += 1
    m_rt_g = aAC.n

    @block.vector
    def _(e):
        e.wait_ge(aAC.h, m_rt_g)
        def step(ins):
            ins.then_inc(vDV.h, 1); vDV.n += 1
            e.wait_ge(vDV.h, vDV.n)
        step(e.tensor_scalar(sm(6), sm(5), -1.0, 1.0, op0=ALU.mult, op1=ALU.add))
        step(e.tensor_copy(s_r2.bitcast(BF16)[:, 0:128], sm(1)))
        step(e.tensor_copy(s_r2.bitcast(BF16)[:, 128:256], sm(3)))
    m_rt_b = vDV.n

    @block.tensor
    def _(e):
        e.wait_ge(vDV.h, m_rt_b)
        e.transpose(pB[:, 0:128], s_r2.bitcast(BF16)[:, 0:128], s_idnb[:]).then_inc(tPE.h, 1); tPE.n += 1
        e.transpose(pB[:, 128:256], s_r2.bitcast(BF16)[:, 128:256], s_idnb[:]).then_inc(tPE.h, 1); tPE.n += 1
        e.transpose(pP[0][:, 0:128], sm(5), s_idnf[:]).then_inc(tPE.h, 1); tPE.n += 1
        e.transpose(pP[0][:, 128:256], sm(6), s_idnf[:]).then_inc(tPE.h, 1); tPE.n += 1
    m_rt_tr = tPE.n
    @block.vector
    def _(e):
        e.wait_ge(tPE.h, m_rt_tr)
        def step(ins):
            ins.then_inc(vDV.h, 1); vDV.n += 1
            e.wait_ge(vDV.h, vDV.n)
        step(e.tensor_copy(s_idxT[:, 0:128], pB[:, 0:128]))
        step(e.tensor_copy(s_idxT[:, 128:256], pB[:, 128:256]))
        step(e.tensor_copy(s_gT[:, 0:128], pP[0][:, 0:128]))
        step(e.tensor_copy(s_gT[:, 128:256], pP[0][:, 128:256]))
        step(e.tensor_tensor(s_m.rearrange("p (u c) -> p u c", c=8),
                             s_idxT.unsqueeze(2).broadcast_to((128, 256, 8)),
                             s_io8b.ap().unsqueeze(1).broadcast_to((128, 256, 8)), ALU.is_equal))
        step(e.tensor_copy(s_idxTf[:], s_idxT[:]))
    m_m = vDV.n

    if PH <= 5:
        st0.close()
        return nc

    @block.tensor
    def _(e):
        e.wait_ge(vDV.h, m_m)
        for qd in range(4):
            e.matmul(pP[qd][:, 0:512], s_sub[:], s_m[:, qd * 512:(qd + 1) * 512], start=True, stop=False)
        e.matmul(pP[4][0:1, 0:512], s_onescol[:], s_m[:, 0:512],
                 start=True, stop=True).then_inc(tPE.h, 1); tPE.n += 1
    tot_pe0 = tPE.n
    tot_dv0 = vDV.n
    tot_sy0 = sRT.n
    for qd in range(1, 4):
        @block.tensor
        def _(e, qd=qd):
            e.wait_ge(vDV.h, tot_dv0 + qd)
            e.matmul(pP[4][0:1, 0:512], s_onescol[:], s_m[:, qd * 512:(qd + 1) * 512],
                     start=True, stop=True).then_inc(tPE.h, 1); tPE.n += 1
    @block.vector
    def _(e):
        for qd in range(4):
            e.wait_ge(tPE.h, tot_pe0 + qd)
            if qd > 0:
                e.wait_ge(sRT.h, tot_sy0 + qd * 16)
            e.tensor_copy(s_t512[:], pP[4][0:1, 0:512]).then_inc(vDV.h, 1); vDV.n += 1
    @block.sync
    def _(e):
        for qd in range(4):
            e.wait_ge(vDV.h, tot_dv0 + qd + 1)
            e.dma_start(out=TROWD.ap()[qd * 512:(qd + 1) * 512, :].rearrange("n a -> a n"),
                        in_=s_t512[:]).then_inc(sRT.h, 16); sRT.n += 16
        e.wait_ge(sRT.h, sRT.n)
        e.dma_start(out=s_TTf.rearrange("p (k c) -> p k c", k=2),
                    in_=TROWD.ap().rearrange("(k p c) a -> p k (c a)", k=2, c=8)).then_inc(sRT.h, 16); sRT.n += 16
    m_TT = sRT.n
    @block.vector
    def _(e):
        e.wait_ge(sRT.h, m_TT)
        e.tensor_copy(s_TT[:], s_TTf[:]).then_inc(vDV.h, 1); vDV.n += 1
    m_TTb = vDV.n

    @block.tensor
    def _(e):
        e.wait_ge(vDV.h, m_TTb)
        e.matmul(pP[6][:, 0:8], s_sub[:], s_TT[:, 0:8], start=True, stop=True)
        e.matmul(pP[4][0:1, 0:8], s_onescol[:], s_TT[:, 0:8], start=True, stop=True).then_inc(tPE.h, 1); tPE.n += 1
    m_C1 = tPE.n
    @block.vector
    def _(e):
        e.wait_ge(tPE.h, m_C1)
        e.tensor_copy(s_carry[:], pP[4][0:1, 0:8]).then_inc(vDV.h, 1); vDV.n += 1
    m_carry = vDV.n
    @block.tensor
    def _(e):
        e.wait_ge(vDV.h, m_carry)
        e.matmul(pP[6][:, 8:16], s_sub[:], s_TT[:, 8:16], start=True, stop=False)
        e.matmul(pP[6][:, 8:16], s_ones1f[:], s_carry[:], start=False, stop=True).then_inc(tPE.h, 1); tPE.n += 1
    m_C2 = tPE.n
    @block.vector
    def _(e):
        e.wait_ge(tPE.h, m_C2)
        e.tensor_copy(s_Cf[:], pP[6][:, 0:16]).then_inc(vDV.h, 1); vDV.n += 1
    m_Cf = vDV.n
    cr_sy0 = sRT.n
    cr_pe0 = tPE.n
    @block.sync
    def _(e):
        e.wait_ge(vDV.h, m_Cf)
        e.dma_start(out=CROWD.ap().rearrange("(k p c) a -> p k (c a)", k=2, c=8),
                    in_=s_Cf.rearrange("p (k c) -> p k c", k=2)).then_inc(sRT.h, 16); sRT.n += 16
        for qd in range(4):
            e.wait_ge(sRT.h, cr_sy0 + 16)
            if qd > 0:
                e.wait_ge(tPE.h, cr_pe0 + qd)
            e.dma_start(out=s_t512[:], in_=CROWD.ap()[qd * 512:(qd + 1) * 512, :].rearrange("n a -> a n")).then_inc(sRT.h, 16); sRT.n += 16
    @block.tensor
    def _(e):
        for qd in range(4):
            e.wait_ge(sRT.h, cr_sy0 + 16 * (qd + 2))
            e.matmul(pP[qd][:, 0:512], s_ones1f[:], s_t512[:],
                     start=False, stop=True).then_inc(tPE.h, 1); tPE.n += 1
    m_P1 = tPE.n

    m_route_q = [0] * 4
    @block.vector
    def _(e):
        def step(ins):
            ins.then_inc(vDV.h, 1); vDV.n += 1
            e.wait_ge(vDV.h, vDV.n)
        for qd in range(4):
            q = slice(qd * 64, (qd + 1) * 64)
            e.wait_ge(tPE.h, cr_pe0 + qd + 1)
            step(e.tensor_tensor(s_pos8[:, qd * 512:(qd + 1) * 512], pP[qd][:, 0:512],
                                 s_m[:, qd * 512:(qd + 1) * 512], ALU.mult))
            step(e.tensor_reduce(s_posE[:, q],
                                 s_pos8.rearrange("p (u c) -> p u c", c=8)[:, q, :],
                                 axis=AX.X, op=ALU.add))
            step(e.tensor_scalar(s_keep[:, q], s_posE[:, q], float(CAP), None, op0=ALU.is_lt))
            step(e.tensor_scalar(s_ra[:, q], s_posE[:, q], 1.0 / 128.0, -0.499, op0=ALU.mult, op1=ALU.add))
            step(e.tensor_copy(s_ru[:, q], s_ra[:, q]))
            step(e.tensor_copy(s_ra[:, q], s_ru[:, q]))
            step(e.tensor_scalar(s_rb[:, q], s_ra[:, q], -128.0, None, op0=ALU.mult))
            step(e.tensor_tensor(s_rb[:, q], s_posE[:, q], s_rb[:, q], ALU.add))
            step(e.tensor_scalar(s_rc[:, q], s_idxTf[:, q], s_myexp[:], None, op0=ALU.is_equal))
            step(e.tensor_tensor(s_rc[:, q], s_rc[:, q], s_keep[:, q], ALU.mult))
            step(e.tensor_scalar(s_rb[:, q], s_rb[:, q], 40.0, None, op0=ALU.mult))
            step(e.tensor_tensor(s_rb[:, q], s_rb[:, q], s_ra[:, q], ALU.add))
            step(e.tensor_scalar(s_rc[:, q], s_rc[:, q], -BIG, BIG, op0=ALU.mult, op1=ALU.add))
            step(e.tensor_tensor(s_rb[:, q], s_rb[:, q], s_rc[:, q], ALU.add))
            step(e.tensor_copy(s_destu[:, q], s_rb[:, q]))
            m_route_q[qd] = vDV.n
        step(e.tensor_scalar(s_ra[:], s_posE[:], 512.0, None, op0=ALU.is_ge))
        for j in (1024.0, 1536.0, 2048.0, 2560.0, 3072.0, 3584.0, 4096.0, 4608.0):
            step(e.tensor_scalar(s_rb[:], s_posE[:], j, None, op0=ALU.is_ge))
            step(e.tensor_tensor(s_ra[:], s_ra[:], s_rb[:], ALU.add))
        step(e.tensor_scalar(s_rb[:], s_ra[:], -512.0, None, op0=ALU.mult))
        step(e.tensor_tensor(s_rb[:], s_posE[:], s_rb[:], ALU.add))
        step(e.tensor_scalar(s_ra[:], s_ra[:], 4096.0, None, op0=ALU.mult))
        step(e.tensor_tensor(s_rb[:], s_rb[:], s_ra[:], ALU.add))
        step(e.tensor_scalar(s_ra[:], s_idxTf[:], 512.0, None, op0=ALU.mult))
        step(e.tensor_tensor(s_rb[:], s_rb[:], s_ra[:], ALU.add))
        step(e.tensor_tensor(s_rb[:], s_rb[:], s_keep[:], ALU.mult))
        step(e.tensor_scalar(s_ra[:], s_keep[:], -float(ZROW), float(ZROW), op0=ALU.mult, op1=ALU.add))
        step(e.tensor_tensor(s_rb[:], s_rb[:], s_ra[:], ALU.add))
    m_route = vDV.n

    @block.tensor
    def _(e):
        e.wait_ge(vDV.h, m_route)
        e.transpose(pP[4][:, 0:128], s_rb[:, 0:128], s_idnf[:]).then_inc(tPE.h, 1); tPE.n += 1
        e.transpose(pP[4][:, 128:256], s_rb[:, 128:256], s_idnf[:]).then_inc(tPE.h, 1); tPE.n += 1
        e.transpose(pP[5][:, 0:128], s_gT[:, 0:128], s_idnf[:]).then_inc(tPE.h, 1); tPE.n += 1
        e.transpose(pP[5][:, 128:256], s_gT[:, 128:256], s_idnf[:]).then_inc(tPE.h, 1); tPE.n += 1
    m_selt = tPE.n
    @block.vector
    def _(e):
        e.wait_ge(tPE.h, m_selt)
        e.tensor_copy(s_pos8[:, 0:256], pP[4][:, 0:256]).then_inc(vDV.h, 1); vDV.n += 1
        e.tensor_copy(s_pos8[:, 256:512], pP[5][:, 0:256]).then_inc(vDV.h, 1); vDV.n += 1
    m_selv = vDV.n
    @block.tensor
    def _(e):
        e.wait_ge(vDV.h, m_selv)
        for src_off, bank in [(0, pP[4]), (256, pP[5])]:
            for uc in range(2):
                ins = e.matmul(bank[:, 0:32],
                               s_pos8[:, src_off + uc * 128: src_off + (uc + 1) * 128],
                               s_sel[:, uc * 32:(uc + 1) * 32],
                               start=(uc == 0), stop=(uc == 1))
        ins.then_inc(tPE.h, 1); tPE.n += 1
    m_selmm = tPE.n
    @block.vector
    def _(e):
        e.wait_ge(tPE.h, m_selmm)
        e.tensor_copy(s_gidxOwnU[:], pP[4][:, 0:32]).then_inc(vDV.h, 1); vDV.n += 1
        e.tensor_copy(s_gOwn[:], pP[5][:, 0:32]).then_inc(vDV.h, 1); vDV.n += 1
    m_own = vDV.n

    @block.gpsimd
    def _(e):
        e.wait_ge(sZ.h, m_sy_zero)
        bound = e.to_reg(SLOTP - 1)
        for u in range(256):
            if u % 64 == 0:
                e.wait_ge(vDV.h, m_route_q[u // 64])
            b = u // 16
            sem = (gSC, gSC2)[b % 2]
            if u % 16 == 0 and b >= 2:
                e.wait_ge(sem.h, sem.n)
            e.indirect_dma_start(out=RT.ap(),
                                 out_offset=IndirectOffsetOnAxis(ap=s_destu[:, u:u + 1], axis=0),
                                 in_=s_tokt[:, u:u + 1], in_offset=None,
                                 bounds_check=bound, oob_is_err=False).then_inc(sem.h, 16); sem.n += 16
    m_scat = gSC.n
    m_scat2 = gSC2.n

    @block.sync
    def _(e):
        e.wait_ge(gSC.h, m_scat)
        e.wait_ge(gSC2.h, m_scat2)
        e.dma_start(out=s_rtl[:], in_=RT.ap().rearrange("(p i) a -> p (i a)", p=128)).then_inc(sRT.h, 16); sRT.n += 16
    m_rtl = sRT.n
    @block.vector
    def _(e):
        e.wait_ge(sRT.h, m_rtl)
        e.tensor_copy(s_rti[:], s_rtl[:]).then_inc(vDV.h, 1); vDV.n += 1
    m_rti = vDV.n

    if PH <= 6:
        st0.close()
        return nc

    T0 = tPE.n
    V0 = vDV.n
    A0 = aAC.n

    PEC, DVC, ACC = 82, 72, 32
    t_trans = lambda c, g: T0 + PEC * c + 8 * (g + 1)
    t_h = lambda c, fc: T0 + PEC * c + 32 + fc + 1
    t_y = lambda c, d2h: T0 + PEC * c + 64 + 9 * (d2h + 1)
    v_trans = lambda c, i: V0 + DVC * c + i + 1
    v_xfm = lambda c: V0 + DVC * c + 32
    v_u = lambda c, fc: V0 + DVC * c + 32 + fc + 1
    v_yev = lambda c, d2h: V0 + DVC * c + 64 + 4 * (d2h + 1)
    a_z = lambda c, fc: A0 + ACC * c + fc + 1
    g_ga = lambda gi: 16 * (gi // 2 + 1)
    ye_ct = lambda c: 64 * (c % 2 + 1)


    def ye_ag(e, c):
        e.wait_ge(sYE[c // 2].h, ye_ct(c))
        e.collective_compute("AllGather", ALU.bypass, replica_groups=[list(range(8))],
                             ins=[YEIN.ap()[c * 512:(c + 1) * 512, :]],
                             outs=[YEOUT.ap()[c * 4096:(c + 1) * 4096, :]]).then_inc(cCC.h, 1); cCC.n += 1
    @block.gpsimd
    def _(e):
        e.wait_ge(vDV.h, m_rti)
        e.wait_ge(cCC.h, m_cc_xn)
        xbound = e.to_reg(T - 1)
        for c in range(NECH):
            for g in range(4):
                gi = c * 4 + g
                if gi >= 2:
                    pc, pg = (gi - 2) // 4, (gi - 2) % 4
                    e.wait_ge(tPE.h, t_trans(pc, pg))
                sem = gXG[gi % 2]
                e.indirect_dma_start(out=s_xtm[:, (gi % 2) * D:(gi % 2 + 1) * D], out_offset=None,
                                     in_=XNOUT.ap(),
                                     in_offset=IndirectOffsetOnAxis(ap=s_rti[:, gi:gi + 1], axis=0),
                                     bounds_check=xbound, oob_is_err=False
                                     ).then_inc(sem.h, 16); sem.n += 16
            if c >= 2:
                ye_ag(e, c - 2)
        for c in (NECH - 2, NECH - 1):
            ye_ag(e, c)
    assert gXG[0].n == 16 * NECH * 2 and gXG[1].n == 16 * NECH * 2
    m_cc_ye = cCC.n

    trslot4 = [pB[:, 0:128], pP[2].ap().bitcast(BF16)[:, 0:128],
               pP[0].ap().bitcast(BF16)[:, 0:128], pP[1].ap().bitcast(BF16)[:, 0:128]]
    for c in range(NECH):
        @block.tensor
        def _(e, c=c):
            if c == 0:
                e.wait_ge(vDV.h, m_own)
            for g in range(4):
                gi = c * 4 + g
                e.wait_ge(gXG[gi % 2].h, g_ga(gi))
                for dc in range(8):
                    i = g * 8 + dc
                    if i % 2 == 0:
                        pi = i - 3
                        if pi >= 0:
                            e.wait_ge(vDV.h, v_trans(c, pi))
                        elif c > 0 and i >= 2:
                            e.wait_ge(vDV.h, v_trans(c - 1, 32 + pi))
                    e.transpose(trslot4[i % 4],
                                s_xtm[:, (gi % 2) * D + dc * 128:(gi % 2) * D + (dc + 1) * 128],
                                s_idnb[:]).then_inc(tPE.h, 1); tPE.n += 1
        assert tPE.n == T0 + PEC * c + 32

        @block.vector
        def _(e, c=c):
            for g in range(4):
                for dc in range(8):
                    i = g * 8 + dc
                    e.wait_ge(tPE.h, T0 + PEC * c + i + 1)
                    e.tensor_copy(s_xfm[:, dc * 512 + g * 128:
                                        dc * 512 + (g + 1) * 128],
                                  trslot4[i % 4]).then_inc(vDV.h, 1); vDV.n += 1
        assert vDV.n == V0 + DVC * c + 32

        @block.tensor
        def _(e, c=c):
            e.wait_ge(vDV.h, v_xfm(c))
            if c == 0:
                e.wait_ge(gW1.h, m_gp_w1)
            else:
                e.wait_ge(vDV.h, v_yev(c - 1, 1))
            for fc in range(32):
                pb = pP[fc % 4]
                if fc % 2 == 0:
                    pfc = fc - 3
                    if pfc >= 0:
                        e.wait_ge(vDV.h, v_u(c, pfc))
                    elif c > 0 and fc >= 2:
                        e.wait_ge(vDV.h, v_u(c - 1, 29 + fc))
                for dc in range(8):
                    ins = e.matmul(pb[:, 0:512],
                                   w1blk(dc, fc * 128, (fc + 1) * 128),
                                   s_xfm[:, dc * 512:(dc + 1) * 512],
                                   start=(dc == 0), stop=(dc == 7))
                ins.then_inc(tPE.h, 1); tPE.n += 1
        assert tPE.n == T0 + PEC * c + 64

        @block.scalar
        def _(e, c=c):
            if c == 0:
                e.wait_ge(vDV.h, m_own)
                e.wait_ge(gSC.h, m_scat)
                e.wait_ge(gSC2.h, m_scat2)
            for fc in range(32):
                pg = fc % 2
                e.wait_ge(vDV.h, v_u(c, fc))
                e.activation(s_hV[:, fc * 512:(fc + 1) * 512], s_hu[:, pg * 512:(pg + 1) * 512],
                             AF.Gelu_apprx_tanh).then_inc(aAC.h, 1); aAC.n += 1
        assert aAC.n == A0 + ACC * c + 32

        @block.vector
        def _(e, c=c):
            for fc in range(32):
                pb = pP[fc % 4]
                pg = fc % 2
                e.wait_ge(tPE.h, t_h(c, fc))
                if fc >= 2:
                    e.wait_ge(aAC.h, a_z(c, fc - 2))
                elif c > 0:
                    e.wait_ge(aAC.h, a_z(c - 1, 30 + fc))
                e.tensor_scalar(s_hu[:, pg * 512:(pg + 1) * 512], pb[:, 0:512],
                                s_b1c[:, fc:fc + 1], None, op0=ALU.add).then_inc(vDV.h, 1); vDV.n += 1
        assert vDV.n == V0 + DVC * c + 64

        @block.tensor
        def _(e, c=c):
            e.wait_ge(aAC.h, a_z(c, 31))
            if c == 0:
                e.wait_ge(vDV.h, m_own)
                e.wait_ge(sROW.h, 128)
                e.wait_ge(vDV.h, m_dv_pre)
                e.wait_ge(gW.h, m_gp_w2)
            else:
                e.wait_ge(vDV.h, v_yev(c - 1, 1))
            for d2h in range(2):
                if d2h == 1:
                    e.wait_ge(vDV.h, v_yev(c, 0))
                for fc in range(32):
                    for tb in range(4):
                        ins = e.matmul(pP[3 + tb][:, 0:512],
                                       s_hV[:, fc * 512 + tb * 128: fc * 512 + (tb + 1) * 128],
                                       w2v(fc)[:, d2h * 512:(d2h + 1) * 512],
                                       start=(fc == 0), stop=False)
                    if fc % 4 == 3:
                        ins.then_inc(tPE.h, 1); tPE.n += 1
                for tb in range(4):
                    ins = e.matmul(pP[3 + tb][:, 0:512], s_ones2b[:, 0:128],
                                   s_b2L[:, d2h * 512:(d2h + 1) * 512], start=False, stop=True)
                ins.then_inc(tPE.h, 1); tPE.n += 1
        assert tPE.n == T0 + PEC * (c + 1)

        @block.vector
        def _(e, c=c):
            for d2h in range(2):
                e.wait_ge(tPE.h, t_y(c, d2h))
                if d2h == 0 and c >= 1:
                    e.wait_ge(sYE[(c - 1) // 2].h, ye_ct(c - 1))
                for tb in range(4):
                    e.tensor_copy(s_yout[:, tb * D + d2h * 512: tb * D + (d2h + 1) * 512],
                                  pP[3 + tb][:, 0:512]).then_inc(vDV.h, 1); vDV.n += 1
        assert vDV.n == V0 + DVC * (c + 1)

        @block.sync
        def _(e, c=c):
            e.wait_ge(vDV.h, v_yev(c, 1))
            sem = sYE[c // 2]
            for tb in range(4):
                rows = slice(c * ECH + tb * 128, c * ECH + (tb + 1) * 128)
                e.dma_start(out=YEIN.ap()[rows, :], in_=s_yout[:, tb * D:(tb + 1) * D]).then_inc(sem.h, 16); sem.n += 16
        assert sYE[c // 2].n == ye_ct(c)

    if PH <= 7:
        st0.close()
        return nc

    fin_dv = []
    out_sy = []
    for tb in range(16):
        pg = tb % 4
        xg = tb % 2
        @block.gpsimd
        def _(e, tb=tb, pg=pg):
            if tb == 0:
                e.wait_ge(cCC.h, m_cc_ye)
                e.wait_ge(vDV.h, m_own)
                e.wait_ge(sZ.h, m_sy_zero)
            ybound = e.to_reg(ZROW)
            if tb >= 4:
                e.wait_ge(vDV.h, fin_dv[tb - 4])
            sem = gFG[tb % 4]
            e.indirect_dma_start(out=s_ga0[:, pg * D:(pg + 1) * D], out_offset=None, in_=YEOUT.ap(),
                                 in_offset=IndirectOffsetOnAxis(ap=s_gidxOwnU[:, tb:tb + 1], axis=0),
                                 bounds_check=ybound, oob_is_err=False
                                 ).then_inc(sem.h, 16); sem.n += 16
            e.indirect_dma_start(out=s_ga1[:, pg * D:(pg + 1) * D], out_offset=None, in_=YEOUT.ap(),
                                 in_offset=IndirectOffsetOnAxis(ap=s_gidxOwnU[:, 16 + tb:17 + tb], axis=0),
                                 bounds_check=ybound, oob_is_err=False
                                 ).then_inc(sem.h, 16); sem.n += 16
        m_ga = gFG[tb % 4].n
        @block.sync
        def _(e, tb=tb, xg=xg):
            if tb < 2:
                e.wait_ge(tPE.h, t_y(NECH - 1, 1))
            else:
                e.wait_ge(vDV.h, fin_dv[tb - 2])
            e.dma_start(out=s_x1r[:, xg * D:(xg + 1) * D],
                        in_=X1D.ap()[tb * 128:(tb + 1) * 128, :]).then_inc(sXR.h, 16); sXR.n += 16
        m_x1r = sXR.n
        @block.vector
        def _(e, tb=tb, pg=pg, xg=xg, m_ga=m_ga, m_x1r=m_x1r):
            e.wait_ge(gFG[tb % 4].h, m_ga)
            e.wait_ge(sXR.h, m_x1r)
            if tb >= 2:
                e.wait_ge(sOUT.h, out_sy[tb - 2])
            def step(ins):
                ins.then_inc(vDV.h, 1); vDV.n += 1
                e.wait_ge(vDV.h, vDV.n)
            s_fb = s_hu.ap().rearrange("p (a b) -> p (a b)", a=2)
            step(e.scalar_tensor_tensor(s_fb, s_ga0[:, pg * D:(pg + 1) * D],
                                        s_gOwn[:, tb:tb + 1], s_x1r[:, xg * D:(xg + 1) * D],
                                        op0=ALU.mult, op1=ALU.add))
            step(e.scalar_tensor_tensor(s_finS[tb % 2], s_ga1[:, pg * D:(pg + 1) * D],
                                        s_gOwn[:, 16 + tb:17 + tb], s_fb,
                                        op0=ALU.mult, op1=ALU.add))
            fin_dv.append(vDV.n)
        @block.sync
        def _(e, tb=tb):
            e.wait_ge(vDV.h, fin_dv[tb])
            e.dma_start(out=out[tb * 128:(tb + 1) * 128, :],
                        in_=s_finS[tb % 2]).then_inc(sOUT.h, 16); sOUT.n += 16
            out_sy.append(sOUT.n)

    m_final = sOUT.n
    @block.sync
    def _(e):
        e.wait_ge(sOUT.h, m_final)

    st0.close()
    return nc


_CACHE = {}


def kernel(**inputs):
    x = np.asarray(inputs["L0_attn_x"], np.float32)
    Wq = np.asarray(inputs["L0_attn_Wq"], np.float32)
    bq = np.asarray(inputs["L0_attn_bq"], np.float32)
    Wk = np.asarray(inputs["L0_attn_Wk"], np.float32)
    bk = np.asarray(inputs["L0_attn_bk"], np.float32)
    Wv = np.asarray(inputs["L0_attn_Wv"], np.float32)
    bv = np.asarray(inputs["L0_attn_bv"], np.float32)
    Wo = np.asarray(inputs["L0_attn_Wo"], np.float32)
    bo = np.asarray(inputs["L0_attn_bo"], np.float32)
    Wg = np.asarray(inputs["L0_ffn_Wg"], np.float32)
    bg = np.asarray(inputs["L0_ffn_bg"], np.float32)
    W1 = np.asarray(inputs["L0_ffn_W1"], np.float32)
    b1 = np.asarray(inputs["L0_ffn_b1"], np.float32)
    W2 = np.asarray(inputs["L0_ffn_W2"], np.float32)
    b2 = np.asarray(inputs["L0_ffn_b2"], np.float32)

    ph = int(os.environ.get("KPH", "8"))
    sub = int(os.environ.get("KSUB", "9"))
    if _CACHE.get("ph") != (ph, sub):
        _CACHE["nc"] = build_nc(ph, sub)
        _CACHE["ph"] = (ph, sub)
    nc = _CACHE["nc"]

    xf = np.ascontiguousarray(x.reshape(NCORES, TC, D))
    wqT = np.ascontiguousarray(Wq.T); wkT = np.ascontiguousarray(Wk.T)
    wvT = np.ascontiguousarray(Wv.T); woT = np.ascontiguousarray(Wo.T)
    idn = np.eye(128, dtype=np.float32)
    su = np.triu(np.ones((128, 128), np.float32), 1)
    ones_r = np.ones((1, 128), np.float32)

    in_maps = []
    for c in range(NCORES):
        sel = np.zeros((256, 32), np.float32)
        for j in range(16):
            sel[c * 16 + j, j] = 1.0
            sel[128 + c * 16 + j, 16 + j] = 1.0
        in_maps.append(dict(
            x_sl=xf[c],
            wqT=wqT, wkT=wkT, wvT=wvT, woT=woT,
            bq_c=np.ascontiguousarray(bq.reshape(D, 1)),
            bk_r=np.ascontiguousarray(bk.reshape(1, D)),
            bv_r=np.ascontiguousarray(bv.reshape(1, D)),
            bo_r=np.ascontiguousarray(bo.reshape(1, D)),
            wg=Wg, bg_r=np.ascontiguousarray(bg.reshape(1, E)),
            w1c=np.ascontiguousarray(W1[c]),
            b1_c=np.ascontiguousarray(b1[c].reshape(FF, 1)),
            w2c=np.ascontiguousarray(W2[c]),
            b2_r=np.ascontiguousarray(b2[c].reshape(1, D)),
            idn=idn, su=su, ones_r=ones_r,
            selmat=sel,
            myexp=np.full((128, 1), float(c), np.float32),
        ))

    res = run_bass_kernel_spmd(nc, in_maps, core_ids=list(range(NCORES)))
    shards = [np.asarray(res.results[c]["out_sl"], np.float32) for c in range(NCORES)]
    return np.concatenate(shards, axis=0).reshape(4, 4096, D)


# version 17
# speedup vs baseline: 1.0276x; 1.0276x over previous
import numpy as np
import os
import contextlib
import concourse.bass as bass
import concourse.mybir as mybir
from concourse.bass import IndirectOffsetOnAxis
from concourse.bass_utils import run_bass_kernel_spmd

F32 = mybir.dt.float32
BF16 = mybir.dt.bfloat16
U32 = mybir.dt.uint32
I32 = mybir.dt.int32
AF = mybir.ActivationFunctionType
ALU = mybir.AluOpType
AX = mybir.AxisListType

NCORES = 8
T = 16384
TC = 2048
D = 1024
NH = 16
DH = 64
E = 8
CAP = 4916
SLOTP = 5120
FF = 4096
NCH = 4
CHT = 512
ECH = 512
NECH = SLOTP // ECH
EPS = 1e-5
BIG = float(2 ** 30)
ZROW = E * SLOTP
GCONST = 0.7978845608028654
GC3 = 0.044715


class Cnt:
    def __init__(self, h):
        self.h = h
        self.n = 0


def build_nc(PH=8, SUB=9):
    nc = bass.Bass("TRN2", target_bir_lowering=False)

    def ein(name, shape):
        return nc.dram_tensor(name, list(shape), F32, kind="ExternalInput").ap()

    x_in = ein("x_sl", (TC, D))
    wqT = ein("wqT", (D, D)); wkT = ein("wkT", (D, D))
    wvT = ein("wvT", (D, D)); woT = ein("woT", (D, D))
    bq_c = ein("bq_c", (D, 1))
    bk_r = ein("bk_r", (1, D)); bv_r = ein("bv_r", (1, D)); bo_r = ein("bo_r", (1, D))
    wg_in = ein("wg", (D, E)); bg_r = ein("bg_r", (1, E))
    w1_in = ein("w1c", (D, FF)); b1_c = ein("b1_c", (FF, 1))
    w2_in = ein("w2c", (FF, D)); b2_r = ein("b2_r", (1, D))
    idn_in = ein("idn", (128, 128))
    su_in = ein("su", (128, 128))
    ones_r = ein("ones_r", (1, 128))
    sel_in = ein("selmat", (256, 32))
    myexp_in = ein("myexp", (128, 1))

    out = nc.dram_tensor("out_sl", [TC, D], F32, kind="ExternalOutput").ap()

    KVIN = nc.dram_tensor("KVIN", [NH * DH, DH], F32, kind="Internal")
    KVOUT = nc.dram_tensor("KVOUT", [2 * NH * DH, DH], F32, kind="Internal")
    LGIN = nc.dram_tensor("LGIN", [TC, E], F32, kind="Internal")
    LGOUT = nc.dram_tensor("LGOUT", [T, E], F32, kind="Internal")
    XNIN = nc.dram_tensor("XNIN", [TC, D], BF16, kind="Internal")
    XNOUT = nc.dram_tensor("XNOUT", [T, D], BF16, kind="Internal")
    X1D = nc.dram_tensor("X1D", [TC, D], F32, kind="Internal")
    RT = nc.dram_tensor("RT", [SLOTP, 1], F32, kind="Internal")
    TROWD = nc.dram_tensor("TROWD", [2048, 1], F32, kind="Internal")
    CROWD = nc.dram_tensor("CROWD", [2048, 1], F32, kind="Internal")
    YEIN = nc.dram_tensor("YEIN", [SLOTP, D], BF16, kind="Internal")
    YEOUT = nc.dram_tensor("YEOUT", [E * SLOTP + 1, D], BF16, kind="Internal")

    st0 = contextlib.ExitStack()
    EE = st0.enter_context

    s_idnf = EE(nc.sbuf_tensor("s_idnf", [128, 128], F32))
    s_idnb = EE(nc.sbuf_tensor("s_idnb", [128, 128], BF16))
    s_sub = EE(nc.sbuf_tensor("s_sub", [128, 128], BF16))
    s_ones1f = EE(nc.sbuf_tensor("s_ones1f", [1, 128], F32))
    s_ones2b = EE(nc.sbuf_tensor("s_ones2b", [2, 128], BF16))
    s_onescol = EE(nc.sbuf_tensor("s_onescol", [128, 1], BF16))
    s_bqc = EE(nc.sbuf_tensor("s_bqc", [128, 8], F32))
    s_negbq = EE(nc.sbuf_tensor("s_negbq", [128, 8], F32))
    s_b1c = EE(nc.sbuf_tensor("s_b1c", [128, 32], F32))
    s_bkL = EE(nc.sbuf_tensor("s_bkL", [2, D], BF16))
    s_bvL = EE(nc.sbuf_tensor("s_bvL", [2, D], BF16))
    s_boL = EE(nc.sbuf_tensor("s_boL", [2, D], BF16))
    s_b2L = EE(nc.sbuf_tensor("s_b2L", [2, D], BF16))
    s_lscr = EE(nc.sbuf_tensor("s_lscr", [1, 512], BF16))
    s_bgrow = EE(nc.sbuf_tensor("s_bgrow", [1, E], F32))
    s_wgt = EE(nc.sbuf_tensor("s_wgt", [128, 64], F32))
    s_myexp = EE(nc.sbuf_tensor("s_myexp", [128, 1], F32))
    s_sel = EE(nc.sbuf_tensor("s_sel", [128, 64], F32))
    s_zf = EE(nc.sbuf_tensor("s_zf", [128, 64], F32))
    s_zb = EE(nc.sbuf_tensor("s_zb", [1, 512], BF16))
    s_cst = EE(nc.sbuf_tensor("s_cst", [128, 2], F32))
    s_io8 = EE(nc.sbuf_tensor("s_io8", [128, 8], F32))
    s_iom8 = EE(nc.sbuf_tensor("s_iom8", [128, 8], F32))
    s_io8b = EE(nc.sbuf_tensor("s_io8b", [128, 8], BF16))
    s_gidxOwnU = EE(nc.sbuf_tensor("s_gidxOwnU", [128, 32], U32))
    s_gOwn = EE(nc.sbuf_tensor("s_gOwn", [128, 32], F32))
    s_rti = EE(nc.sbuf_tensor("s_rti", [128, 40], U32))

    pP = [EE(nc.psum_tensor("pP%d" % i, [128, 512], F32)) for i in range(7)]
    pB = EE(nc.psum_tensor("pB", [128, 512], BF16))

    sIN = Cnt(EE(nc.semaphore("sIN")))
    sROW = Cnt(EE(nc.semaphore("sROW")))
    sROW2 = Cnt(EE(nc.semaphore("sROW2")))
    sZ = Cnt(EE(nc.semaphore("sZ")))
    sX = Cnt(EE(nc.semaphore("sX")))
    sKV = Cnt(EE(nc.semaphore("sKV")))
    sST = Cnt(EE(nc.semaphore("sST")))
    sRT = Cnt(EE(nc.semaphore("sRT")))
    sYE = [Cnt(EE(nc.semaphore("sYE%d" % j))) for j in range(5)]
    sXR = Cnt(EE(nc.semaphore("sXR")))
    sOUT = Cnt(EE(nc.semaphore("sOUT")))
    gK = Cnt(EE(nc.semaphore("gK")))
    gW = Cnt(EE(nc.semaphore("gW")))
    gW1 = Cnt(EE(nc.semaphore("gW1")))
    gSC = Cnt(EE(nc.semaphore("gSC")))
    gSC2 = Cnt(EE(nc.semaphore("gSC2")))
    gXG = [Cnt(EE(nc.semaphore("gXGa"))), Cnt(EE(nc.semaphore("gXGb")))]
    gFG = [Cnt(EE(nc.semaphore("gFG%d" % j))) for j in range(4)]
    tPE = Cnt(EE(nc.semaphore("tPE")))
    vDV = Cnt(EE(nc.semaphore("vDV")))
    aAC = Cnt(EE(nc.semaphore("aAC")))
    cCC = Cnt(EE(nc.semaphore("cCC")))
    gPC = Cnt(EE(nc.semaphore("gPC")))
    ALLC = [sIN, sROW, sROW2, sZ, sX, sKV, sST, sRT] + sYE + [
        sXR, sOUT, gK, gW, gW1, gSC, gSC2, gXG[0], gXG[1]] + gFG + [
        tPE, vDV, aAC, cCC, gPC]

    block = EE(nc.Block())

    def waitall(e, *cnts):
        for c in (cnts or ALLC):
            if c.n > 0:
                e.wait_ge(c.h, c.n)

    s_xt = EE(nc.sbuf_tensor("s_xt", [128, 4 * D], F32))
    s_qf = EE(nc.sbuf_tensor("s_qf", [128, 8 * TC], BF16))
    s_brow = s_qf.ap().bitcast(F32)[0:1, 0:1024]
    @block.sync
    def _(e):
        for tb in range(4):
            e.dma_start(out=s_xt[:, tb * D:(tb + 1) * D],
                        in_=x_in[tb * 128:(tb + 1) * 128, :]).then_inc(sX.h, 16); sX.n += 16
    m_x_ch0 = sX.n

    @block.sync
    def _(e):
        def d(o, i):
            e.dma_start(out=o, in_=i).then_inc(sIN.h, 16); sIN.n += 16
        d(s_idnf[:], idn_in[:])
        for fc in range(8):
            d(s_bqc[:, fc:fc + 1], bq_c[fc * 128:(fc + 1) * 128, :])
        for fc in range(32):
            d(s_b1c[:, fc:fc + 1], b1_c[fc * 128:(fc + 1) * 128, :])
        d(s_bgrow[:], bg_r[:])
        for dc in range(8):
            d(s_wgt[:, dc * 8:(dc + 1) * 8], wg_in[dc * 128:(dc + 1) * 128, :])
        d(s_myexp[:], myexp_in[:])
        for uc in range(2):
            d(s_sel[:, uc * 32:(uc + 1) * 32], sel_in[uc * 128:(uc + 1) * 128, :])
        d(s_ones1f[:], ones_r[:])
    m_sy_pre = sIN.n

    @block.gpsimd
    def _(e):
        def d(o, i):
            e.dma_start(out=o, in_=i).then_inc(gK.h, 16); gK.n += 16
        d(s_idnb[:], idn_in[:])
        d(s_sub[:], su_in[:])
        d(s_ones2b[0:1, :], ones_r[:])
        d(s_ones2b[1:2, :], ones_r[:])
        d(s_onescol[:], ones_r.rearrange("a p -> p a"))
        e.memset(s_zf[:], 0.0).then_inc(gPC.h, 1); gPC.n += 1
        e.memset(s_zb[:], 0.0).then_inc(gPC.h, 1); gPC.n += 1
        e.memset(s_cst[:, 0:1], EPS).then_inc(gPC.h, 1); gPC.n += 1
        e.memset(s_cst[:, 1:2], 0.5).then_inc(gPC.h, 1); gPC.n += 1
    m_gp_consts = gK.n
    m_gpc_consts = gPC.n

    dv_limb0 = vDV.n
    row_ms = lambda i: 16 * (i + 1)
    cp_ms = lambda i, h: 16 * (2 * i + h + 1)
    dv_ms = lambda i, k: dv_limb0 + 3 * i + k
    @block.sync
    def _(e):
        rows = [bk_r, bv_r, bo_r, b2_r]
        dsts = [s_bkL, s_bvL, s_boL, s_b2L]
        e.dma_start(out=s_brow[:], in_=rows[0][:]).then_inc(sROW2.h, 16); sROW2.n += 16
        for i in range(4):
            for h in range(2):
                e.wait_ge(vDV.h, dv_ms(i, 2 + h))
                e.dma_start(out=dsts[i][1:2, h * 512:(h + 1) * 512], in_=s_lscr[:]).then_inc(sROW.h, 16); sROW.n += 16
            if i < 3:
                e.wait_ge(vDV.h, dv_ms(i, 3))
                e.dma_start(out=s_brow[:], in_=rows[i + 1][:]).then_inc(sROW2.h, 16); sROW2.n += 16
    assert sROW.n == 8 * 16 and sROW2.n == 4 * 16
    @block.vector
    def _(e):
        e.wait_ge(sIN.h, m_sy_pre)
        for i, dst in enumerate([s_bkL, s_bvL, s_boL, s_b2L]):
            e.wait_ge(sROW2.h, row_ms(i))
            e.tensor_copy(dst[0:1, :], s_brow[:]).then_inc(vDV.h, 1); vDV.n += 1
            for h in range(2):
                if i + h > 0:
                    prev = cp_ms(i, h - 1) if h == 1 else cp_ms(i - 1, 1)
                    e.wait_ge(sROW.h, prev)
                e.wait_ge(vDV.h, vDV.n)
                e.tensor_tensor(s_lscr[:], s_brow[:, h * 512:(h + 1) * 512],
                                dst[0:1, h * 512:(h + 1) * 512], ALU.subtract).then_inc(vDV.h, 1); vDV.n += 1
        e.tensor_scalar(s_negbq[:], s_bqc[:], -1.0, None, op0=ALU.mult).then_inc(vDV.h, 1); vDV.n += 1
    m_dv_pre = vDV.n

    @block.sync
    def _(e):
        e.wait_ge(gK.h, m_gp_consts)
        e.wait_ge(gPC.h, m_gpc_consts)
        e.dma_start(out=RT.ap().rearrange("(p a) b -> p (a b)", p=128),
                    in_=s_zf[:, 0:40]).then_inc(sZ.h, 16); sZ.n += 16
        e.dma_start(out=YEOUT.ap()[ZROW:ZROW + 1, 0:512], in_=s_zb[:]).then_inc(sZ.h, 16); sZ.n += 16
        e.dma_start(out=YEOUT.ap()[ZROW:ZROW + 1, 512:1024], in_=s_zb[:]).then_inc(sZ.h, 16); sZ.n += 16
    m_sy_zero = sZ.n

    if PH <= 1:
        st0.close()
        return nc

    s_wqT = EE(nc.sbuf_tensor("s_wqT", [128, 8 * D], BF16))
    s_wkT = EE(nc.sbuf_tensor("s_wkT", [128, 8 * D], BF16))
    s_wvT = EE(nc.sbuf_tensor("s_wvT", [128, 8 * D], BF16))
    s_woT = EE(nc.sbuf_tensor("s_woT", [128, 8 * D], BF16))
    s_xnf = EE(nc.sbuf_tensor("s_xnf", [128, 8 * TC], BF16))
    s_k = EE(nc.sbuf_tensor("s_k", [128, 4 * D], BF16))
    s_v = EE(nc.sbuf_tensor("s_v", [128, 4 * D], BF16))
    s_stat = EE(nc.sbuf_tensor("s_stat", [128, 8], F32))
    s_sq = EE(nc.sbuf_tensor("s_sq", [128, D], F32))
    s_kvb = EE(nc.sbuf_tensor("s_kvb", [128, 1024], BF16))
    s_attnf = EE(nc.sbuf_tensor("s_attnf", [128, 8 * CHT], BF16))
    s_x1 = EE(nc.sbuf_tensor("s_x1", [128, D], F32))
    s_kvf = s_x1.ap()[0:64, :]
    s_kv2 = s_attnf.ap().bitcast(F32)[0:64, 0:1024]
    KB32 = s_k.ap().bitcast(F32)
    VB32 = s_v.ap().bitcast(F32)
    s_xn1b = s_v.ap()[:, 0:1024]
    s_xn1f = KB32[:, 0:1024]
    s_xn1T = KB32[:, 1024:2048]
    s_lgt = VB32[:, 512:520]

    @block.gpsimd
    def _(e):
        for dst, src in [(s_wqT, wqT), (s_wkT, wkT), (s_wvT, wvT), (s_woT, woT)]:
            for ec in range(8):
                e.dma_start(out=dst[:, ec * D:(ec + 1) * D],
                            in_=src[ec * 128:(ec + 1) * 128, :]).then_inc(gW.h, 16); gW.n += 16
    m_gp_wattn = gW.n

    for ch in range(NCH):
        if ch == 0:
            m_x = m_x_ch0
        else:
            @block.sync
            def _(e, ch=ch):
                waitall(e, vDV, tPE)
                for tb in range(4):
                    e.dma_start(out=s_xt[:, tb * D:(tb + 1) * D],
                                in_=x_in[ch * CHT + tb * 128:ch * CHT + (tb + 1) * 128, :]).then_inc(sX.h, 16); sX.n += 16
            m_x = sX.n

        @block.scalar
        def _(e):
            e.wait_ge(sX.h, m_x)
            for tb in range(4):
                if tb > 0:
                    e.wait_ge(aAC.h, aAC.n)
                e.activation(s_sq[:], s_xt[:, tb * D:(tb + 1) * D], AF.Square,
                             accum_out=s_stat[:, 2 * tb + 1:2 * tb + 2]).then_inc(aAC.h, 1); aAC.n += 1
        m_sq = aAC.n
        @block.vector
        def _(e):
            e.wait_ge(sX.h, m_x)
            for tb in range(4):
                e.tensor_reduce(s_stat[:, 2 * tb:2 * tb + 1], s_xt[:, tb * D:(tb + 1) * D],
                                axis=AX.X, op=ALU.add).then_inc(vDV.h, 1); vDV.n += 1
            e.wait_ge(aAC.h, m_sq)
            e.wait_ge(vDV.h, vDV.n)
            for tb in range(4):
                m = s_stat[:, 2 * tb:2 * tb + 1]
                sq = s_stat[:, 2 * tb + 1:2 * tb + 2]
                e.tensor_scalar(m, m, 1.0 / D, None, op0=ALU.mult).then_inc(vDV.h, 1); vDV.n += 1
                e.wait_ge(vDV.h, vDV.n)
                e.tensor_tensor(s_sq[:, 0:1], m, m, ALU.mult).then_inc(vDV.h, 1); vDV.n += 1
                e.wait_ge(vDV.h, vDV.n)
                e.tensor_scalar(sq, sq, 1.0 / D, s_sq[:, 0:1], op0=ALU.mult, op1=ALU.subtract).then_inc(vDV.h, 1); vDV.n += 1
                e.wait_ge(vDV.h, vDV.n)
        m_var = vDV.n
        @block.scalar
        def _(e, ch=ch):
            e.wait_ge(vDV.h, m_var)
            if ch == 0:
                e.wait_ge(gPC.h, m_gpc_consts)
            for tb in range(4):
                sq = s_stat[:, 2 * tb + 1:2 * tb + 2]
                e.activation(sq, sq, AF.Sqrt, bias=s_cst[:, 0:1]).then_inc(aAC.h, 1); aAC.n += 1
        m_sqrt = aAC.n
        @block.vector
        def _(e):
            e.wait_ge(aAC.h, m_sqrt)
            for tb in range(4):
                m = s_stat[:, 2 * tb:2 * tb + 1]
                sq = s_stat[:, 2 * tb + 1:2 * tb + 2]
                e.reciprocal(sq, sq).then_inc(vDV.h, 1); vDV.n += 1
                e.wait_ge(vDV.h, vDV.n)
                e.tensor_tensor(m, m, sq, ALU.mult).then_inc(vDV.h, 1); vDV.n += 1
                e.wait_ge(vDV.h, vDV.n)
                e.tensor_scalar(s_k[:, tb * D:(tb + 1) * D], s_xt[:, tb * D:(tb + 1) * D],
                                sq, m, op0=ALU.mult, op1=ALU.subtract).then_inc(vDV.h, 1); vDV.n += 1
        m_xn = vDV.n

        if SUB < 2:
            continue
        tr_pe0 = tPE.n
        tr_dv0 = vDV.n
        trslot1 = [pB[:, 0:128], pP[6].ap().bitcast(BF16)[:, 0:128]]
        @block.tensor
        def _(e, ch=ch):
            e.wait_ge(vDV.h, m_xn)
            e.wait_ge(gK.h, m_gp_consts)
            for i in range(32):
                tb, ec = i // 8, i % 8
                if i >= 2:
                    e.wait_ge(vDV.h, tr_dv0 + i - 1)
                e.transpose(trslot1[i % 2],
                            s_k[:, tb * D + ec * 128: tb * D + (ec + 1) * 128],
                            s_idnb[:]).then_inc(tPE.h, 1); tPE.n += 1
        assert tPE.n == tr_pe0 + 32
        @block.vector
        def _(e, ch=ch):
            for i in range(32):
                tb, ec = i // 8, i % 8
                e.wait_ge(tPE.h, tr_pe0 + i + 1)
                e.tensor_copy(s_xnf[:, ec * TC + ch * CHT + tb * 128: ec * TC + ch * CHT + (tb + 1) * 128],
                              trslot1[i % 2]).then_inc(vDV.h, 1); vDV.n += 1
        assert vDV.n == tr_dv0 + 32
        m_xnf = vDV.n

        if SUB < 3:
            continue
        kv_pe0 = tPE.n
        kv_dv0 = vDV.n
        kv_ac0 = aAC.n
        k_u = lambda idx: kv_dv0 + 2 * idx + 1
        k_ev = lambda idx: kv_dv0 + 2 * idx + 2
        v_ev = lambda idx: kv_dv0 + 16 + idx + 1
        k_ac = lambda idx: kv_ac0 + 2 * idx + 2
        k_pe = lambda idx: kv_pe0 + idx + 1
        v_pe = lambda idx: kv_pe0 + 8 + idx + 1
        @block.tensor
        def _(e, ch=ch):
            e.wait_ge(vDV.h, m_xnf)
            e.wait_ge(gW.h, m_gp_wattn)
            e.wait_ge(vDV.h, m_dv_pre)
            e.wait_ge(sROW.h, 128)
            for which, W, bias, banks, evf in [("k", s_wkT, s_bkL, (pP[0], pP[1]), k_ev),
                                               ("v", s_wvT, s_bvL, (pP[2], pP[3]), v_ev)]:
                for idx in range(8):
                    tb, hf = idx // 2, idx % 2
                    pb = banks[idx % 2]
                    if idx >= 2:
                        e.wait_ge(vDV.h, evf(idx - 2))
                    ins = None
                    for ec in range(8):
                        e.matmul(pb[:, 0:512],
                                 s_xnf[:, ec * TC + ch * CHT + tb * 128: ec * TC + ch * CHT + (tb + 1) * 128],
                                 W[:, ec * D + hf * 512: ec * D + (hf + 1) * 512],
                                 start=(ec == 0), stop=False)
                    e.matmul(pb[:, 0:512], s_ones2b[:, 0:128], bias[:, hf * 512:(hf + 1) * 512],
                             start=False, stop=True).then_inc(tPE.h, 1); tPE.n += 1
        assert tPE.n == kv_pe0 + 16
        @block.scalar
        def _(e, ch=ch):
            for idx in range(8):
                e.wait_ge(vDV.h, k_u(idx))
                if idx > 0:
                    e.wait_ge(aAC.h, aAC.n)
                    e.wait_ge(vDV.h, k_ev(idx - 1))
                e.activation(s_sq[:, 512:1024], s_sq[:, 0:512], AF.Relu, scale=-1.0).then_inc(aAC.h, 1); aAC.n += 1
                e.wait_ge(aAC.h, aAC.n)
                e.activation(s_sq[:, 512:1024], s_sq[:, 512:1024], AF.Exp, scale=-1.0).then_inc(aAC.h, 1); aAC.n += 1
        assert aAC.n == kv_ac0 + 16
        @block.vector
        def _(e, ch=ch):
            for idx in range(8):
                tb, hf = idx // 2, idx % 2
                pb = (pP[0], pP[1])[idx % 2]
                e.wait_ge(tPE.h, k_pe(idx))
                if idx > 0:
                    e.wait_ge(aAC.h, k_ac(idx - 1))
                e.wait_ge(vDV.h, vDV.n)
                e.tensor_copy(s_sq[:, 0:512], pb[:, 0:512]).then_inc(vDV.h, 1); vDV.n += 1
                e.wait_ge(aAC.h, k_ac(idx))
                e.wait_ge(vDV.h, vDV.n)
                e.scalar_tensor_tensor(s_k[:, tb * D + hf * 512: tb * D + (hf + 1) * 512],
                                       s_sq[:, 0:512], 0.0, s_sq[:, 512:1024],
                                       op0=ALU.max, op1=ALU.add).then_inc(vDV.h, 1); vDV.n += 1
            for idx in range(8):
                tb, hf = idx // 2, idx % 2
                pb = (pP[2], pP[3])[idx % 2]
                e.wait_ge(tPE.h, v_pe(idx))
                e.tensor_copy(s_v[:, tb * D + hf * 512: tb * D + (hf + 1) * 512], pb[:, 0:512]).then_inc(vDV.h, 1); vDV.n += 1
        assert vDV.n == kv_dv0 + 24
        m_kvev = vDV.n

        if SUB < 4:
            continue
        @block.tensor
        def _(e, ch=ch):
            e.wait_ge(vDV.h, m_kvev)
            for tb in range(4):
                for h in range(NH):
                    bank = pP[4] if h < 8 else pP[5]
                    col = (h % 8) * 64
                    ins = e.matmul(bank[0:64, col:col + 64],
                                   s_k[:, tb * D + h * DH: tb * D + (h + 1) * DH],
                                   s_v[:, tb * D + h * DH: tb * D + (h + 1) * DH],
                                   start=(ch == 0 and tb == 0 and (h % 8) == 0),
                                   stop=(ch == NCH - 1 and tb == 3 and (h % 8) == 7),
                                   skip_group_check=True)
            ins.then_inc(tPE.h, 1); tPE.n += 1
    m_kv_pe = tPE.n

    if SUB < 4:
        st0.close()
        return nc

    @block.vector
    def _(e):
        e.wait_ge(tPE.h, m_kv_pe)
        e.tensor_copy(s_kvf[:, 0:512], pP[4][0:64, :]).then_inc(vDV.h, 1); vDV.n += 1
        e.tensor_copy(s_kvf[:, 512:1024], pP[5][0:64, :]).then_inc(vDV.h, 1); vDV.n += 1
    m_kvf = vDV.n

    if PH <= 2:
        st0.close()
        return nc
    @block.sync
    def _(e):
        e.wait_ge(vDV.h, m_kvf)
        e.dma_start(out=KVIN.ap().rearrange("(h d) e -> d h e", h=16),
                    in_=s_kvf.rearrange("d (h e) -> d h e", e=64)).then_inc(sKV.h, 16); sKV.n += 16
    m_kvin = sKV.n
    @block.gpsimd
    def _(e):
        e.wait_ge(sKV.h, m_kvin)
        e.collective_compute("AllGather", ALU.bypass,
                             replica_groups=[[0, 1], [2, 3], [4, 5], [6, 7]],
                             ins=[KVIN.ap()], outs=[KVOUT.ap()]).then_inc(cCC.h, 1); cCC.n += 1
    m_cc_kv = cCC.n

    q_pe0 = tPE.n
    q_dv0 = vDV.n
    q_ac0 = aAC.n
    q_pe = lambda idx: q_pe0 + idx + 1
    q_u = lambda idx: q_dv0 + 2 * idx + 1
    q_ev = lambda idx: q_dv0 + 2 * idx + 2
    q_ac = lambda idx: q_ac0 + 2 * idx + 2
    q_dv_base0 = vDV.n
    @block.tensor
    def _(e):
        e.wait_ge(vDV.h, q_dv_base0)
        for idx in range(32):
            ch, fc = idx // 8, idx % 8
            if idx >= 2:
                e.wait_ge(vDV.h, q_ev(idx - 2))
                e.wait_ge(aAC.h, q_ac(idx - 2))
            for ec in range(8):
                ins = e.matmul((pP[6], pP[3])[idx % 2][:, 0:512],
                               s_wqT[:, ec * D + fc * 128: ec * D + (fc + 1) * 128],
                               s_xnf[:, ec * TC + ch * CHT: ec * TC + (ch + 1) * CHT],
                               start=(ec == 0), stop=(ec == 7))
            ins.then_inc(tPE.h, 1); tPE.n += 1
    assert tPE.n == q_pe0 + 32
    @block.scalar
    def _(e):
        for idx in range(32):
            e.wait_ge(vDV.h, q_u(idx))
            if idx > 0:
                e.wait_ge(aAC.h, aAC.n)
                e.wait_ge(vDV.h, q_ev(idx - 1))
            e.activation(s_sq[:, 512:1024], s_sq[:, 0:512], AF.Relu, scale=-1.0).then_inc(aAC.h, 1); aAC.n += 1
            e.wait_ge(aAC.h, aAC.n)
            e.activation(s_sq[:, 512:1024], s_sq[:, 512:1024], AF.Exp, scale=-1.0).then_inc(aAC.h, 1); aAC.n += 1
    assert aAC.n == q_ac0 + 64
    @block.vector
    def _(e):
        for idx in range(32):
            ch, fc = idx // 8, idx % 8
            e.wait_ge(tPE.h, q_pe(idx))
            if idx == 0:
                e.wait_ge(sROW2.h, 64)
            if idx > 0:
                e.wait_ge(aAC.h, q_ac(idx - 1))
            e.wait_ge(vDV.h, vDV.n)
            e.tensor_scalar(s_sq[:, 0:512], (pP[6], pP[3])[idx % 2][:, 0:512], s_bqc[:, fc:fc + 1], None,
                            op0=ALU.add).then_inc(vDV.h, 1); vDV.n += 1
            e.wait_ge(aAC.h, q_ac(idx))
            e.wait_ge(vDV.h, vDV.n)
            e.scalar_tensor_tensor(s_qf[:, fc * TC + ch * CHT: fc * TC + (ch + 1) * CHT],
                                   s_sq[:, 0:512], 0.0, s_sq[:, 512:1024],
                                   op0=ALU.max, op1=ALU.add).then_inc(vDV.h, 1); vDV.n += 1
    assert vDV.n == q_dv0 + 64
    m_q = vDV.n

    m_q_pe = q_pe0 + 32
    XTB = s_xt.ap().bitcast(BF16)
    SQB = s_sq.ap().bitcast(BF16)
    X1B = s_x1.ap().bitcast(BF16)

    def w2v(fc):
        if fc < 8:
            return s_wkT[:, fc * 1024:(fc + 1) * 1024]
        if fc < 16:
            return s_wvT[:, (fc - 8) * 1024:(fc - 7) * 1024]
        if fc < 24:
            return s_wqT[:, (fc - 16) * 1024:(fc - 15) * 1024]
        return s_qf[:, (fc - 24) * 1024:(fc - 23) * 1024]

    def w1blk(dc, a, b):
        if dc in (0, 1):
            return s_qf[:, 8192 + dc * FF + a:8192 + dc * FF + b]
        if dc in (2, 3):
            return XTB[:, (dc - 2) * FF + a:(dc - 2) * FF + b]
        if dc == 4:
            return s_attnf[:, a:b]
        if dc == 5:
            if b <= 2048:
                return SQB[:, a:b]
            return X1B[:, a - 2048:b - 2048]
        return s_woT[:, (dc - 6) * FF + a:(dc - 6) * FF + b]

    @block.sync
    def _(e):
        e.wait_ge(cCC.h, m_cc_kv)
        e.dma_start(out=s_kvf.rearrange("d (h e) -> d h e", e=64),
                    in_=KVOUT.ap()[0:1024, :].rearrange("(h d) e -> d h e", h=16)).then_inc(sKV.h, 16); sKV.n += 16
        e.dma_start(out=s_kv2.rearrange("d (h e) -> d h e", e=64),
                    in_=KVOUT.ap()[1024:2048, :].rearrange("(h d) e -> d h e", h=16)).then_inc(sKV.h, 16); sKV.n += 16
    m_kv2 = sKV.n
    @block.vector
    def _(e):
        e.wait_ge(sKV.h, m_kv2)
        e.tensor_tensor(s_kvb[0:64, :], s_kvf[:], s_kv2[:], ALU.add).then_inc(vDV.h, 1); vDV.n += 1
        e.wait_ge(vDV.h, vDV.n)
        e.tensor_copy(s_kvb[64:128, :], s_kvb[0:64, :]).then_inc(vDV.h, 1); vDV.n += 1
    m_kvb = vDV.n

    if PH <= 3:
        st0.close()
        return nc

    prev_lgv = prev_x1 = prev_sq2 = 0
    for ch in range(NCH):
        @block.sync
        def _(e, ch=ch, prev_x1=prev_x1):
            if ch == 0:
                waitall(e, vDV, tPE, aAC)
            else:
                e.wait_ge(vDV.h, prev_x1)
            for tb in range(4):
                e.dma_start(out=s_xt[:, tb * D:(tb + 1) * D],
                            in_=x_in[ch * CHT + tb * 128:ch * CHT + (tb + 1) * 128, :]).then_inc(sX.h, 16); sX.n += 16
        m_x2 = sX.n

        at_pe0 = tPE.n
        at_dv0 = vDV.n
        @block.tensor
        def _(e, ch=ch, prev_lgv=prev_lgv):
            e.wait_ge(vDV.h, max(m_kvb, m_q))
            if ch > 0:
                e.wait_ge(vDV.h, prev_lgv)
            e.wait_ge(sX.h, m_x2)
            for hp in range(8):
                pb = pP[hp % 2]
                if hp >= 2:
                    e.wait_ge(vDV.h, at_dv0 + hp - 1)
                for sub in range(2):
                    h = 2 * hp + sub
                    hb = (h % 2) * 64
                    ins = e.matmul(pb[hb:hb + 64, 0:512],
                                   s_kvb[hb:hb + 64, h * 64:(h + 1) * 64],
                                   s_qf[hb:hb + 64, (h // 2) * TC + ch * CHT: (h // 2) * TC + (ch + 1) * CHT],
                                   start=True, stop=True, skip_group_check=True)
                ins.then_inc(tPE.h, 1); tPE.n += 1
        assert tPE.n == at_pe0 + 8
        @block.vector
        def _(e, ch=ch):
            for hp in range(8):
                pb = pP[hp % 2]
                e.wait_ge(tPE.h, at_pe0 + hp + 1)
                e.tensor_copy(s_attnf[:, hp * 512:(hp + 1) * 512], pb[:, 0:512]).then_inc(vDV.h, 1); vDV.n += 1
        assert vDV.n == at_dv0 + 8
        m_attnf = vDV.n

        MO = [0] * 4; MX1 = [0] * 4; MSQ2 = [0] * 4; MXN1 = [0] * 4
        MTRE = [0] * 4; MLGST = [0] * 4

        def obank(tb, hf):
            return pP[2 + 2 * (tb % 2) + hf]

        def emit_o(tb, ch=ch):
            @block.tensor
            def _(e, ch=ch, tb=tb):
                e.wait_ge(vDV.h, m_attnf)
                if tb == 0:
                    e.wait_ge(sROW.h, 128)
                if tb >= 2:
                    e.wait_ge(vDV.h, MX1[tb - 2])
                for hf in range(2):
                    pb = obank(tb, hf)
                    for fc in range(8):
                        e.matmul(pb[:, 0:512],
                                 s_attnf[:, fc * 512 + tb * 128: fc * 512 + (tb + 1) * 128],
                                 s_woT[:, fc * D + hf * 512: fc * D + (hf + 1) * 512],
                                 start=(fc == 0), stop=False)
                    ins = e.matmul(pb[:, 0:512], s_ones2b[:, 0:128], s_boL[:, hf * 512:(hf + 1) * 512],
                                   start=False, stop=True)
                    ins.then_inc(tPE.h, 1); tPE.n += 1
            MO[tb] = tPE.n

        emit_o(0)
        emit_o(1)
        for tb in range(4):
            m_o = MO[tb]
            @block.vector
            def _(e, ch=ch, tb=tb, m_o=m_o):
                e.wait_ge(tPE.h, m_o)
                e.wait_ge(sST.h, sST.n)
                if tb >= 1:
                    e.wait_ge(aAC.h, MSQ2[tb - 1])
                for hf in range(2):
                    e.tensor_tensor(s_x1[:, hf * 512:(hf + 1) * 512], obank(tb, hf)[:, 0:512],
                                    s_xt[:, tb * D + hf * 512: tb * D + (hf + 1) * 512], ALU.add).then_inc(vDV.h, 1); vDV.n += 1
                e.wait_ge(vDV.h, vDV.n)
                e.tensor_reduce(s_stat[:, 0:1], s_x1[:], axis=AX.X, op=ALU.add).then_inc(vDV.h, 1); vDV.n += 1
            m_x1 = vDV.n; MX1[tb] = m_x1
            @block.scalar
            def _(e, ch=ch, tb=tb, m_x1=m_x1):
                e.wait_ge(vDV.h, m_x1)
                e.wait_ge(aAC.h, aAC.n)
                e.activation(s_sq[:], s_x1[:], AF.Square, accum_out=s_stat[:, 1:2]).then_inc(aAC.h, 1); aAC.n += 1
            m_sq2 = aAC.n; MSQ2[tb] = m_sq2
            @block.vector
            def _(e, ch=ch, tb=tb, m_sq2=m_sq2):
                e.wait_ge(aAC.h, m_sq2)
                e.tensor_scalar(s_stat[:, 0:1], s_stat[:, 0:1], 1.0 / D, None, op0=ALU.mult).then_inc(vDV.h, 1); vDV.n += 1
                e.wait_ge(vDV.h, vDV.n)
                e.tensor_tensor(s_stat[:, 4:5], s_stat[:, 0:1], s_stat[:, 0:1], ALU.mult).then_inc(vDV.h, 1); vDV.n += 1
                e.wait_ge(vDV.h, vDV.n)
                e.tensor_scalar(s_stat[:, 2:3], s_stat[:, 1:2], 1.0 / D, s_stat[:, 4:5],
                                op0=ALU.mult, op1=ALU.subtract).then_inc(vDV.h, 1); vDV.n += 1
            m_var2 = vDV.n
            @block.scalar
            def _(e, ch=ch, tb=tb, m_var2=m_var2):
                e.wait_ge(vDV.h, m_var2)
                e.activation(s_stat[:, 3:4], s_stat[:, 2:3], AF.Sqrt, bias=s_cst[:, 0:1]).then_inc(aAC.h, 1); aAC.n += 1
            m_sqrt2 = aAC.n
            @block.vector
            def _(e, ch=ch, tb=tb, m_sqrt2=m_sqrt2):
                e.wait_ge(aAC.h, m_sqrt2)
                if tb >= 1:
                    e.wait_ge(tPE.h, MTRE[tb - 1])
                e.reciprocal(s_stat[:, 3:4], s_stat[:, 3:4]).then_inc(vDV.h, 1); vDV.n += 1
                e.wait_ge(vDV.h, vDV.n)
                e.tensor_tensor(s_stat[:, 5:6], s_stat[:, 0:1], s_stat[:, 3:4], ALU.mult).then_inc(vDV.h, 1); vDV.n += 1
                e.wait_ge(vDV.h, vDV.n)
                e.tensor_scalar(s_xn1b, s_x1[:], s_stat[:, 3:4], s_stat[:, 5:6],
                                op0=ALU.mult, op1=ALU.subtract).then_inc(vDV.h, 1); vDV.n += 1
                e.tensor_scalar(s_xn1f, s_x1[:], s_stat[:, 3:4], s_stat[:, 5:6],
                                op0=ALU.mult, op1=ALU.subtract).then_inc(vDV.h, 1); vDV.n += 1
            m_xn1 = vDV.n; MXN1[tb] = m_xn1
            @block.sync
            def _(e, ch=ch, tb=tb, m_x1=m_x1, m_xn1=m_xn1):
                rows = slice(ch * CHT + tb * 128, ch * CHT + (tb + 1) * 128)
                e.wait_ge(vDV.h, m_x1)
                e.dma_start(out=X1D.ap()[rows, :], in_=s_x1[:]).then_inc(sST.h, 16); sST.n += 16
                e.wait_ge(vDV.h, m_xn1)
                e.dma_start(out=XNIN.ap()[rows, :], in_=s_xn1b).then_inc(sST.h, 16); sST.n += 16

            tr2_pe0 = tPE.n
            tr2_dv0 = vDV.n
            @block.tensor
            def _(e, ch=ch, tb=tb, m_xn1=m_xn1, tr2_dv0=tr2_dv0):
                e.wait_ge(vDV.h, m_xn1)
                for dc in range(8):
                    if dc >= 2:
                        e.wait_ge(vDV.h, tr2_dv0 + dc - 1)
                    e.transpose(pP[dc % 2][:, 0:128],
                                s_xn1f[:, dc * 128:(dc + 1) * 128], s_idnf[:]).then_inc(tPE.h, 1); tPE.n += 1
            assert tPE.n == tr2_pe0 + 8
            MTRE[tb] = tPE.n
            @block.vector
            def _(e, ch=ch, tb=tb, tr2_pe0=tr2_pe0):
                for dc in range(8):
                    e.wait_ge(tPE.h, tr2_pe0 + dc + 1)
                    e.tensor_copy(s_xn1T[:, dc * 128:(dc + 1) * 128],
                                  pP[dc % 2][:, 0:128]).then_inc(vDV.h, 1); vDV.n += 1
            assert vDV.n == tr2_dv0 + 8
            m_xn1T = vDV.n
            @block.tensor
            def _(e, ch=ch, tb=tb, m_xn1T=m_xn1T):
                e.wait_ge(vDV.h, m_xn1T)
                for dc in range(8):
                    e.matmul(pP[1][:, 256:264],
                             s_xn1T[:, dc * 128:(dc + 1) * 128],
                             s_wgt[:, dc * 8:(dc + 1) * 8],
                             start=(dc == 0), stop=False)
                e.matmul(pP[1][:, 256:264], s_ones1f[:], s_bgrow[:], start=False, stop=True).then_inc(tPE.h, 1); tPE.n += 1
            m_lgp = tPE.n
            if tb + 2 < 4:
                emit_o(tb + 2)
            @block.vector
            def _(e, ch=ch, tb=tb, m_lgp=m_lgp):
                e.wait_ge(tPE.h, m_lgp)
                if tb >= 1:
                    e.wait_ge(sST.h, MLGST[tb - 1])
                e.tensor_copy(s_lgt, pP[1][:, 256:264]).then_inc(vDV.h, 1); vDV.n += 1
            m_lgv = vDV.n
            @block.sync
            def _(e, ch=ch, tb=tb, m_lgv=m_lgv):
                e.wait_ge(vDV.h, m_lgv)
                rows = slice(ch * CHT + tb * 128, ch * CHT + (tb + 1) * 128)
                e.dma_start(out=LGIN.ap()[rows, :], in_=s_lgt).then_inc(sST.h, 16); sST.n += 16
            MLGST[tb] = sST.n
        m_o = MO[3]; m_x1 = MX1[3]; m_sq2 = MSQ2[3]; m_xn1 = MXN1[3]
        prev_lgv = m_lgv
        prev_x1 = MX1[3]; prev_sq2 = MSQ2[3]

    m_attn_sy = sST.n

    if PH <= 4:
        st0.close()
        return nc

    @block.gpsimd
    def _(e):
        e.wait_ge(tPE.h, m_kv_pe)
        for fc in range(16):
            e.dma_start(out=w2v(fc),
                        in_=w2_in[fc * 128:(fc + 1) * 128, :]).then_inc(gW.h, 16); gW.n += 16
        e.wait_ge(tPE.h, m_q_pe)
        for fc in range(16, 24):
            e.dma_start(out=w2v(fc),
                        in_=w2_in[fc * 128:(fc + 1) * 128, :]).then_inc(gW.h, 16); gW.n += 16
        e.wait_ge(tPE.h, at_pe0 + 8)
        for fc in range(24, 32):
            e.dma_start(out=w2v(fc),
                        in_=w2_in[fc * 128:(fc + 1) * 128, :]).then_inc(gW.h, 16); gW.n += 16
        for dc in (0, 1):
            e.dma_start(out=w1blk(dc, 0, FF),
                        in_=w1_in[dc * 128:(dc + 1) * 128, :]).then_inc(gW1.h, 16); gW1.n += 16
        e.wait_ge(tPE.h, m_o)
        for dc in (4, 6, 7):
            e.dma_start(out=w1blk(dc, 0, FF),
                        in_=w1_in[dc * 128:(dc + 1) * 128, :]).then_inc(gW1.h, 16); gW1.n += 16
        e.wait_ge(vDV.h, m_x1)
        for dc in (2, 3):
            e.dma_start(out=w1blk(dc, 0, FF),
                        in_=w1_in[dc * 128:(dc + 1) * 128, :]).then_inc(gW1.h, 16); gW1.n += 16
        e.wait_ge(aAC.h, m_sq2)
        e.dma_start(out=w1blk(5, 0, 2048),
                    in_=w1_in[5 * 128:6 * 128, 0:2048]).then_inc(gW1.h, 16); gW1.n += 16
        e.wait_ge(vDV.h, m_xn1)
        e.wait_ge(sST.h, m_attn_sy)
        e.dma_start(out=w1blk(5, 2048, FF),
                    in_=w1_in[5 * 128:6 * 128, 2048:4096]).then_inc(gW1.h, 16); gW1.n += 16
    m_gp_w1 = gW1.n
    m_gp_w2 = gW.n

    @block.gpsimd
    def _(e):
        e.wait_ge(sST.h, m_attn_sy)
        e.collective_compute("AllGather", ALU.bypass, replica_groups=[list(range(8))],
                             ins=[LGIN.ap()], outs=[LGOUT.ap()]).then_inc(cCC.h, 1); cCC.n += 1
        e.collective_compute("AllGather", ALU.bypass, replica_groups=[list(range(8))],
                             ins=[XNIN.ap()], outs=[XNOUT.ap()]).then_inc(cCC.h, 1); cCC.n += 1
    m_cc_lg = cCC.n - 1
    m_cc_xn = cCC.n

    BAR = dict(t=tPE.n, v=vDV.n, a=aAC.n, s=sST.n)

    s_xtm = EE(nc.sbuf_tensor("s_xtm", [128, 2 * D], BF16))
    s_xfm = EE(nc.sbuf_tensor("s_xfm", [128, 8 * ECH], BF16))
    s_hu = EE(nc.sbuf_tensor("s_hu", [128, 2 * 512], F32))

    X32 = s_xnf.ap().bitcast(F32)
    s_L = X32[:, 0:1024]
    s_r1 = X32[:, 1024:2048]
    s_r2 = X32[:, 2048:3072]
    s_small = X32[:, 3072:4096]
    s_pos8 = X32[:, 4096:6144]
    s_tokt = X32[:, 6144:6400]
    s_idxTf = X32[:, 6400:6656]
    s_gT = X32[:, 6656:6912]
    s_posE = X32[:, 6912:7168]
    s_keep = X32[:, 7168:7424]
    s_ra = X32[:, 7424:7680]
    s_rb = X32[:, 7680:7936]
    s_rc = X32[:, 7936:8192]
    s_ioint = s_xnf.ap().bitcast(I32)[:, 4096:6144]
    s_m = s_xfm.ap()[:, 0:2048]
    s_idxT = s_xfm.ap()[:, 2048:2304]
    VU = s_v.ap().bitcast(U32)
    s_ru = VU[:, 0:256]
    s_destu = VU[:, 256:512]
    s_t512 = VB32[0:1, 512:1024]
    s_TTf = VB32[:, 1024:1040]
    s_carry = VB32[0:1, 1040:1048]
    s_Cf = VB32[:, 1048:1064]
    s_rtl = VB32[:, 1064:1104]
    s_TT = s_v.ap()[:, 2208:2224]

    s_hV = s_xnf.ap()
    s_yout = s_k.ap()
    s_ga0 = s_xnf.ap()[:, 0:4096]
    s_ga1 = s_xnf.ap()[:, 4096:8192]
    s_fin = X32[:, 4096:5120]
    s_fb = X32[:, 5120:6144]
    s_x1r = X32[:, 6144:8192]

    @block.gpsimd
    def _(e):
        e.wait_ge(tPE.h, BAR['t']); e.wait_ge(vDV.h, BAR['v'])
        e.wait_ge(aAC.h, BAR['a']); e.wait_ge(sST.h, BAR['s'])
        e.iota(s_ioint[:, 0:8], [[1, 8]], base=-8, channel_multiplier=0).then_inc(gPC.h, 1); gPC.n += 1
        e.wait_ge(gPC.h, gPC.n)
        e.tensor_copy(s_iom8[:], s_ioint[:, 0:8]).then_inc(gPC.h, 1); gPC.n += 1
        e.wait_ge(gPC.h, gPC.n)
        e.tensor_scalar(s_io8[:], s_iom8[:], 8.0, None, op0=ALU.add).then_inc(gPC.h, 1); gPC.n += 1
        e.wait_ge(gPC.h, gPC.n)
        e.tensor_copy(s_io8b[:], s_io8[:]).then_inc(gPC.h, 1); gPC.n += 1
        e.wait_ge(gPC.h, gPC.n)
        e.iota(s_ioint[:, 0:256], [[0, 2], [128, 128]], base=0, channel_multiplier=1).then_inc(gPC.h, 1); gPC.n += 1
        e.wait_ge(gPC.h, gPC.n)
        e.tensor_copy(s_tokt[:], s_ioint[:, 0:256]).then_inc(gPC.h, 1); gPC.n += 1
    m_gpc_route = gPC.n

    @block.sync
    def _(e):
        e.wait_ge(cCC.h, m_cc_lg)
        e.wait_ge(tPE.h, BAR['t']); e.wait_ge(vDV.h, BAR['v'])
        e.dma_start(out=s_L[:], in_=LGOUT.ap().rearrange("(p j) c -> p (j c)", p=128)).then_inc(sRT.h, 16); sRT.n += 16
    m_L = sRT.n

    CW = 128
    def sm(i):
        return s_small[:, i * CW:(i + 1) * CW]
    L3 = s_L.rearrange("p (j c) -> p j c", c=8)
    R13 = s_r1.rearrange("p (j c) -> p j c", c=8)
    R23 = s_r2.rearrange("p (j c) -> p j c", c=8)

    @block.vector
    def _(e):
        e.wait_ge(sRT.h, m_L)
        e.wait_ge(gPC.h, m_gpc_route)
        def step(ins):
            ins.then_inc(vDV.h, 1); vDV.n += 1
            e.wait_ge(vDV.h, vDV.n)
        step(e.tensor_reduce(sm(0), L3, axis=AX.X, op=ALU.max))
        step(e.tensor_tensor(R13, L3, sm(0).unsqueeze(2).broadcast_to((128, 128, 8)), ALU.is_equal))
        step(e.tensor_tensor(R23, R13, s_iom8.ap().unsqueeze(1).broadcast_to((128, 128, 8)), ALU.mult))
        step(e.tensor_reduce(sm(1), R23, axis=AX.X, op=ALU.min))
        step(e.tensor_scalar(sm(1), sm(1), 8.0, None, op0=ALU.add))
        step(e.tensor_tensor(R13, s_io8.ap().unsqueeze(1).broadcast_to((128, 128, 8)), sm(1).unsqueeze(2).broadcast_to((128, 128, 8)), ALU.is_equal))
        step(e.tensor_scalar(s_r1[:], s_r1[:], -1e30, None, op0=ALU.mult))
        step(e.tensor_tensor(s_r1[:], s_r1[:], s_L[:], ALU.add))
        step(e.tensor_reduce(sm(2), R13, axis=AX.X, op=ALU.max))
        step(e.tensor_tensor(R13, R13, sm(2).unsqueeze(2).broadcast_to((128, 128, 8)), ALU.is_equal))
        step(e.tensor_tensor(R23, R13, s_iom8.ap().unsqueeze(1).broadcast_to((128, 128, 8)), ALU.mult))
        step(e.tensor_reduce(sm(3), R23, axis=AX.X, op=ALU.min))
        step(e.tensor_scalar(sm(3), sm(3), 8.0, None, op0=ALU.add))
        step(e.tensor_tensor(sm(4), sm(0), sm(2), ALU.subtract))
    m_rt_a = vDV.n

    @block.scalar
    def _(e):
        e.wait_ge(vDV.h, m_rt_a)
        e.activation(sm(5), sm(4), AF.Sigmoid).then_inc(aAC.h, 1); aAC.n += 1
    m_rt_g = aAC.n

    @block.vector
    def _(e):
        e.wait_ge(aAC.h, m_rt_g)
        def step(ins):
            ins.then_inc(vDV.h, 1); vDV.n += 1
            e.wait_ge(vDV.h, vDV.n)
        step(e.tensor_scalar(sm(6), sm(5), -1.0, 1.0, op0=ALU.mult, op1=ALU.add))
        step(e.tensor_copy(s_r2.bitcast(BF16)[:, 0:128], sm(1)))
        step(e.tensor_copy(s_r2.bitcast(BF16)[:, 128:256], sm(3)))
    m_rt_b = vDV.n

    @block.tensor
    def _(e):
        e.wait_ge(vDV.h, m_rt_b)
        e.transpose(pB[:, 0:128], s_r2.bitcast(BF16)[:, 0:128], s_idnb[:]).then_inc(tPE.h, 1); tPE.n += 1
        e.transpose(pB[:, 128:256], s_r2.bitcast(BF16)[:, 128:256], s_idnb[:]).then_inc(tPE.h, 1); tPE.n += 1
        e.transpose(pP[0][:, 0:128], sm(5), s_idnf[:]).then_inc(tPE.h, 1); tPE.n += 1
        e.transpose(pP[0][:, 128:256], sm(6), s_idnf[:]).then_inc(tPE.h, 1); tPE.n += 1
    m_rt_tr = tPE.n
    @block.vector
    def _(e):
        e.wait_ge(tPE.h, m_rt_tr)
        def step(ins):
            ins.then_inc(vDV.h, 1); vDV.n += 1
            e.wait_ge(vDV.h, vDV.n)
        step(e.tensor_copy(s_idxT[:, 0:128], pB[:, 0:128]))
        step(e.tensor_copy(s_idxT[:, 128:256], pB[:, 128:256]))
        step(e.tensor_copy(s_gT[:, 0:128], pP[0][:, 0:128]))
        step(e.tensor_copy(s_gT[:, 128:256], pP[0][:, 128:256]))
        step(e.tensor_tensor(s_m.rearrange("p (u c) -> p u c", c=8),
                             s_idxT.unsqueeze(2).broadcast_to((128, 256, 8)),
                             s_io8b.ap().unsqueeze(1).broadcast_to((128, 256, 8)), ALU.is_equal))
        step(e.tensor_copy(s_idxTf[:], s_idxT[:]))
    m_m = vDV.n

    if PH <= 5:
        st0.close()
        return nc

    mv3 = s_m.rearrange("p (u c) -> p u c", c=8)
    @block.tensor
    def _(e):
        e.wait_ge(vDV.h, m_m)
        for qd in range(4):
            e.matmul(pP[qd][:, 0:512], s_sub[:], s_m[:, qd * 512:(qd + 1) * 512], start=True, stop=False)
        ins = None
        for k in range(2):
            for cc in range(8):
                ins = e.matmul(pP[4][:, k * 8 + cc:k * 8 + cc + 1],
                               mv3[:, k * 128:(k + 1) * 128, cc:cc + 1],
                               s_onescol[:], start=True, stop=True,
                               skip_group_check=True)
        ins.then_inc(tPE.h, 1); tPE.n += 1
    m_totT = tPE.n
    @block.vector
    def _(e):
        e.wait_ge(tPE.h, m_totT)
        e.tensor_copy(s_TTf[:], pP[4][:, 0:16]).then_inc(vDV.h, 1); vDV.n += 1
        e.wait_ge(vDV.h, vDV.n)
        e.tensor_copy(s_TT[:], s_TTf[:]).then_inc(vDV.h, 1); vDV.n += 1
    m_TTb = vDV.n

    @block.tensor
    def _(e):
        e.wait_ge(vDV.h, m_TTb)
        e.matmul(pP[6][:, 0:8], s_sub[:], s_TT[:, 0:8], start=True, stop=True)
        e.matmul(pP[4][0:1, 0:8], s_onescol[:], s_TT[:, 0:8], start=True, stop=True).then_inc(tPE.h, 1); tPE.n += 1
    m_C1 = tPE.n
    @block.vector
    def _(e):
        e.wait_ge(tPE.h, m_C1)
        e.tensor_copy(s_carry[:], pP[4][0:1, 0:8]).then_inc(vDV.h, 1); vDV.n += 1
    m_carry = vDV.n
    @block.tensor
    def _(e):
        e.wait_ge(vDV.h, m_carry)
        e.matmul(pP[6][:, 8:16], s_sub[:], s_TT[:, 8:16], start=True, stop=False)
        e.matmul(pP[6][:, 8:16], s_ones1f[:], s_carry[:], start=False, stop=True).then_inc(tPE.h, 1); tPE.n += 1
    m_C2 = tPE.n
    @block.vector
    def _(e):
        e.wait_ge(tPE.h, m_C2)
        e.tensor_copy(s_Cf[:], pP[6][:, 0:16]).then_inc(vDV.h, 1); vDV.n += 1
    m_Cf = vDV.n
    cr_sy0 = sRT.n
    cr_pe0 = tPE.n
    @block.sync
    def _(e):
        e.wait_ge(vDV.h, m_Cf)
        e.dma_start(out=CROWD.ap().rearrange("(k p c) a -> p k (c a)", k=2, c=8),
                    in_=s_Cf.rearrange("p (k c) -> p k c", k=2)).then_inc(sRT.h, 16); sRT.n += 16
        for qd in range(4):
            e.wait_ge(sRT.h, cr_sy0 + 16)
            if qd > 0:
                e.wait_ge(tPE.h, cr_pe0 + qd)
            e.dma_start(out=s_t512[:], in_=CROWD.ap()[qd * 512:(qd + 1) * 512, :].rearrange("n a -> a n")).then_inc(sRT.h, 16); sRT.n += 16
    @block.tensor
    def _(e):
        for qd in range(4):
            e.wait_ge(sRT.h, cr_sy0 + 16 * (qd + 2))
            e.matmul(pP[qd][:, 0:512], s_ones1f[:], s_t512[:],
                     start=False, stop=True).then_inc(tPE.h, 1); tPE.n += 1
    m_P1 = tPE.n

    m_route_q = [0] * 4
    @block.vector
    def _(e):
        def step(ins):
            ins.then_inc(vDV.h, 1); vDV.n += 1
            e.wait_ge(vDV.h, vDV.n)
        for qd in range(4):
            q = slice(qd * 64, (qd + 1) * 64)
            e.wait_ge(tPE.h, cr_pe0 + qd + 1)
            step(e.tensor_tensor(s_pos8[:, qd * 512:(qd + 1) * 512], pP[qd][:, 0:512],
                                 s_m[:, qd * 512:(qd + 1) * 512], ALU.mult))
            step(e.tensor_reduce(s_posE[:, q],
                                 s_pos8.rearrange("p (u c) -> p u c", c=8)[:, q, :],
                                 axis=AX.X, op=ALU.add))
            step(e.tensor_scalar(s_keep[:, q], s_posE[:, q], float(CAP), None, op0=ALU.is_lt))
            step(e.tensor_scalar(s_ra[:, q], s_posE[:, q], 1.0 / 128.0, -0.499, op0=ALU.mult, op1=ALU.add))
            step(e.tensor_copy(s_ru[:, q], s_ra[:, q]))
            step(e.tensor_copy(s_ra[:, q], s_ru[:, q]))
            step(e.tensor_scalar(s_rb[:, q], s_ra[:, q], -128.0, None, op0=ALU.mult))
            step(e.tensor_tensor(s_rb[:, q], s_posE[:, q], s_rb[:, q], ALU.add))
            step(e.tensor_scalar(s_rc[:, q], s_idxTf[:, q], s_myexp[:], None, op0=ALU.is_equal))
            step(e.tensor_tensor(s_rc[:, q], s_rc[:, q], s_keep[:, q], ALU.mult))
            step(e.tensor_scalar(s_rb[:, q], s_rb[:, q], 40.0, None, op0=ALU.mult))
            step(e.tensor_tensor(s_rb[:, q], s_rb[:, q], s_ra[:, q], ALU.add))
            step(e.tensor_scalar(s_rc[:, q], s_rc[:, q], -BIG, BIG, op0=ALU.mult, op1=ALU.add))
            step(e.tensor_tensor(s_rb[:, q], s_rb[:, q], s_rc[:, q], ALU.add))
            step(e.tensor_copy(s_destu[:, q], s_rb[:, q]))
            m_route_q[qd] = vDV.n
        step(e.tensor_scalar(s_ra[:], s_posE[:], 512.0, None, op0=ALU.is_ge))
        for j in (1024.0, 1536.0, 2048.0, 2560.0, 3072.0, 3584.0, 4096.0, 4608.0):
            step(e.tensor_scalar(s_rb[:], s_posE[:], j, None, op0=ALU.is_ge))
            step(e.tensor_tensor(s_ra[:], s_ra[:], s_rb[:], ALU.add))
        step(e.tensor_scalar(s_rb[:], s_ra[:], -512.0, None, op0=ALU.mult))
        step(e.tensor_tensor(s_rb[:], s_posE[:], s_rb[:], ALU.add))
        step(e.tensor_scalar(s_ra[:], s_ra[:], 4096.0, None, op0=ALU.mult))
        step(e.tensor_tensor(s_rb[:], s_rb[:], s_ra[:], ALU.add))
        step(e.tensor_scalar(s_ra[:], s_idxTf[:], 512.0, None, op0=ALU.mult))
        step(e.tensor_tensor(s_rb[:], s_rb[:], s_ra[:], ALU.add))
        step(e.tensor_tensor(s_rb[:], s_rb[:], s_keep[:], ALU.mult))
        step(e.tensor_scalar(s_ra[:], s_keep[:], -float(ZROW), float(ZROW), op0=ALU.mult, op1=ALU.add))
        step(e.tensor_tensor(s_rb[:], s_rb[:], s_ra[:], ALU.add))
    m_route = vDV.n

    @block.tensor
    def _(e):
        e.wait_ge(vDV.h, m_route)
        e.transpose(pP[4][:, 0:128], s_rb[:, 0:128], s_idnf[:]).then_inc(tPE.h, 1); tPE.n += 1
        e.transpose(pP[4][:, 128:256], s_rb[:, 128:256], s_idnf[:]).then_inc(tPE.h, 1); tPE.n += 1
        e.transpose(pP[5][:, 0:128], s_gT[:, 0:128], s_idnf[:]).then_inc(tPE.h, 1); tPE.n += 1
        e.transpose(pP[5][:, 128:256], s_gT[:, 128:256], s_idnf[:]).then_inc(tPE.h, 1); tPE.n += 1
    m_selt = tPE.n
    @block.vector
    def _(e):
        e.wait_ge(tPE.h, m_selt)
        e.tensor_copy(s_pos8[:, 0:256], pP[4][:, 0:256]).then_inc(vDV.h, 1); vDV.n += 1
        e.tensor_copy(s_pos8[:, 256:512], pP[5][:, 0:256]).then_inc(vDV.h, 1); vDV.n += 1
    m_selv = vDV.n
    @block.tensor
    def _(e):
        e.wait_ge(vDV.h, m_selv)
        for src_off, bank in [(0, pP[4]), (256, pP[5])]:
            for uc in range(2):
                ins = e.matmul(bank[:, 0:32],
                               s_pos8[:, src_off + uc * 128: src_off + (uc + 1) * 128],
                               s_sel[:, uc * 32:(uc + 1) * 32],
                               start=(uc == 0), stop=(uc == 1))
        ins.then_inc(tPE.h, 1); tPE.n += 1
    m_selmm = tPE.n
    @block.vector
    def _(e):
        e.wait_ge(tPE.h, m_selmm)
        e.tensor_copy(s_gidxOwnU[:], pP[4][:, 0:32]).then_inc(vDV.h, 1); vDV.n += 1
        e.tensor_copy(s_gOwn[:], pP[5][:, 0:32]).then_inc(vDV.h, 1); vDV.n += 1
    m_own = vDV.n

    @block.gpsimd
    def _(e):
        e.wait_ge(sZ.h, m_sy_zero)
        bound = e.to_reg(SLOTP - 1)
        for u in range(256):
            if u % 64 == 0:
                e.wait_ge(vDV.h, m_route_q[u // 64])
            b = u // 16
            sem = (gSC, gSC2)[b % 2]
            if u % 16 == 0 and b >= 2:
                e.wait_ge(sem.h, sem.n)
            e.indirect_dma_start(out=RT.ap(),
                                 out_offset=IndirectOffsetOnAxis(ap=s_destu[:, u:u + 1], axis=0),
                                 in_=s_tokt[:, u:u + 1], in_offset=None,
                                 bounds_check=bound, oob_is_err=False).then_inc(sem.h, 16); sem.n += 16
    m_scat = gSC.n
    m_scat2 = gSC2.n

    @block.sync
    def _(e):
        e.wait_ge(gSC.h, m_scat)
        e.wait_ge(gSC2.h, m_scat2)
        e.dma_start(out=s_rtl[:], in_=RT.ap().rearrange("(p i) a -> p (i a)", p=128)).then_inc(sRT.h, 16); sRT.n += 16
    m_rtl = sRT.n
    @block.vector
    def _(e):
        e.wait_ge(sRT.h, m_rtl)
        e.tensor_copy(s_rti[:], s_rtl[:]).then_inc(vDV.h, 1); vDV.n += 1
    m_rti = vDV.n

    if PH <= 6:
        st0.close()
        return nc

    T0 = tPE.n
    V0 = vDV.n
    A0 = aAC.n

    PEC, DVC, ACC = 82, 72, 32
    t_trans = lambda c, g: T0 + PEC * c + 8 * (g + 1)
    t_h = lambda c, fc: T0 + PEC * c + 32 + fc + 1
    t_y = lambda c, d2h: T0 + PEC * c + 64 + 9 * (d2h + 1)
    v_trans = lambda c, i: V0 + DVC * c + i + 1
    v_xfm = lambda c: V0 + DVC * c + 32
    v_u = lambda c, fc: V0 + DVC * c + 32 + fc + 1
    v_yev = lambda c, d2h: V0 + DVC * c + 64 + 4 * (d2h + 1)
    a_z = lambda c, fc: A0 + ACC * c + fc + 1
    g_ga = lambda gi: 16 * (gi // 2 + 1)
    ye_ct = lambda c: 64 * (c % 2 + 1)


    def ye_ag(e, c):
        e.wait_ge(sYE[c // 2].h, ye_ct(c))
        e.collective_compute("AllGather", ALU.bypass, replica_groups=[list(range(8))],
                             ins=[YEIN.ap()[c * 512:(c + 1) * 512, :]],
                             outs=[YEOUT.ap()[c * 4096:(c + 1) * 4096, :]]).then_inc(cCC.h, 1); cCC.n += 1
    @block.gpsimd
    def _(e):
        e.wait_ge(vDV.h, m_rti)
        e.wait_ge(cCC.h, m_cc_xn)
        xbound = e.to_reg(T - 1)
        for c in range(NECH):
            for g in range(4):
                gi = c * 4 + g
                if gi >= 2:
                    pc, pg = (gi - 2) // 4, (gi - 2) % 4
                    e.wait_ge(tPE.h, t_trans(pc, pg))
                sem = gXG[gi % 2]
                e.indirect_dma_start(out=s_xtm[:, (gi % 2) * D:(gi % 2 + 1) * D], out_offset=None,
                                     in_=XNOUT.ap(),
                                     in_offset=IndirectOffsetOnAxis(ap=s_rti[:, gi:gi + 1], axis=0),
                                     bounds_check=xbound, oob_is_err=False
                                     ).then_inc(sem.h, 16); sem.n += 16
            if c >= 2:
                ye_ag(e, c - 2)
        for c in (NECH - 2, NECH - 1):
            ye_ag(e, c)
    assert gXG[0].n == 16 * NECH * 2 and gXG[1].n == 16 * NECH * 2
    m_cc_ye = cCC.n

    trslot4 = [pB[:, 0:128], pP[2].ap().bitcast(BF16)[:, 0:128],
               pP[0].ap().bitcast(BF16)[:, 0:128], pP[1].ap().bitcast(BF16)[:, 0:128]]
    for c in range(NECH):
        @block.tensor
        def _(e, c=c):
            if c == 0:
                e.wait_ge(vDV.h, m_own)
            for g in range(4):
                gi = c * 4 + g
                e.wait_ge(gXG[gi % 2].h, g_ga(gi))
                for dc in range(8):
                    i = g * 8 + dc
                    if i % 2 == 0:
                        pi = i - 3
                        if pi >= 0:
                            e.wait_ge(vDV.h, v_trans(c, pi))
                        elif c > 0 and i >= 2:
                            e.wait_ge(vDV.h, v_trans(c - 1, 32 + pi))
                    e.transpose(trslot4[i % 4],
                                s_xtm[:, (gi % 2) * D + dc * 128:(gi % 2) * D + (dc + 1) * 128],
                                s_idnb[:]).then_inc(tPE.h, 1); tPE.n += 1
        assert tPE.n == T0 + PEC * c + 32

        @block.vector
        def _(e, c=c):
            for g in range(4):
                for dc in range(8):
                    i = g * 8 + dc
                    e.wait_ge(tPE.h, T0 + PEC * c + i + 1)
                    e.tensor_copy(s_xfm[:, dc * 512 + g * 128:
                                        dc * 512 + (g + 1) * 128],
                                  trslot4[i % 4]).then_inc(vDV.h, 1); vDV.n += 1
        assert vDV.n == V0 + DVC * c + 32

        @block.tensor
        def _(e, c=c):
            e.wait_ge(vDV.h, v_xfm(c))
            if c == 0:
                e.wait_ge(gW1.h, m_gp_w1)
            else:
                e.wait_ge(vDV.h, v_yev(c - 1, 1))
            for fc in range(32):
                pb = pP[fc % 4]
                if fc % 2 == 0:
                    pfc = fc - 3
                    if pfc >= 0:
                        e.wait_ge(vDV.h, v_u(c, pfc))
                    elif c > 0 and fc >= 2:
                        e.wait_ge(vDV.h, v_u(c - 1, 29 + fc))
                for dc in range(8):
                    ins = e.matmul(pb[:, 0:512],
                                   w1blk(dc, fc * 128, (fc + 1) * 128),
                                   s_xfm[:, dc * 512:(dc + 1) * 512],
                                   start=(dc == 0), stop=(dc == 7))
                ins.then_inc(tPE.h, 1); tPE.n += 1
        assert tPE.n == T0 + PEC * c + 64

        @block.scalar
        def _(e, c=c):
            if c == 0:
                e.wait_ge(vDV.h, m_own)
                e.wait_ge(gSC.h, m_scat)
                e.wait_ge(gSC2.h, m_scat2)
            for fc in range(32):
                pg = fc % 2
                e.wait_ge(vDV.h, v_u(c, fc))
                e.activation(s_hV[:, fc * 512:(fc + 1) * 512], s_hu[:, pg * 512:(pg + 1) * 512],
                             AF.Gelu_apprx_tanh).then_inc(aAC.h, 1); aAC.n += 1
        assert aAC.n == A0 + ACC * c + 32

        @block.vector
        def _(e, c=c):
            for fc in range(32):
                pb = pP[fc % 4]
                pg = fc % 2
                e.wait_ge(tPE.h, t_h(c, fc))
                if fc >= 2:
                    e.wait_ge(aAC.h, a_z(c, fc - 2))
                elif c > 0:
                    e.wait_ge(aAC.h, a_z(c - 1, 30 + fc))
                e.tensor_scalar(s_hu[:, pg * 512:(pg + 1) * 512], pb[:, 0:512],
                                s_b1c[:, fc:fc + 1], None, op0=ALU.add).then_inc(vDV.h, 1); vDV.n += 1
        assert vDV.n == V0 + DVC * c + 64

        @block.tensor
        def _(e, c=c):
            e.wait_ge(aAC.h, a_z(c, 31))
            if c == 0:
                e.wait_ge(vDV.h, m_own)
                e.wait_ge(sROW.h, 128)
                e.wait_ge(vDV.h, m_dv_pre)
                e.wait_ge(gW.h, m_gp_w2)
            else:
                e.wait_ge(vDV.h, v_yev(c - 1, 1))
            for d2h in range(2):
                if d2h == 1:
                    e.wait_ge(vDV.h, v_yev(c, 0))
                for fc in range(32):
                    for tb in range(4):
                        ins = e.matmul(pP[3 + tb][:, 0:512],
                                       s_hV[:, fc * 512 + tb * 128: fc * 512 + (tb + 1) * 128],
                                       w2v(fc)[:, d2h * 512:(d2h + 1) * 512],
                                       start=(fc == 0), stop=False)
                    if fc % 4 == 3:
                        ins.then_inc(tPE.h, 1); tPE.n += 1
                for tb in range(4):
                    ins = e.matmul(pP[3 + tb][:, 0:512], s_ones2b[:, 0:128],
                                   s_b2L[:, d2h * 512:(d2h + 1) * 512], start=False, stop=True)
                ins.then_inc(tPE.h, 1); tPE.n += 1
        assert tPE.n == T0 + PEC * (c + 1)

        @block.vector
        def _(e, c=c):
            for d2h in range(2):
                e.wait_ge(tPE.h, t_y(c, d2h))
                if d2h == 0 and c >= 1:
                    e.wait_ge(sYE[(c - 1) // 2].h, ye_ct(c - 1))
                for tb in range(4):
                    e.tensor_copy(s_yout[:, tb * D + d2h * 512: tb * D + (d2h + 1) * 512],
                                  pP[3 + tb][:, 0:512]).then_inc(vDV.h, 1); vDV.n += 1
        assert vDV.n == V0 + DVC * (c + 1)

        @block.sync
        def _(e, c=c):
            e.wait_ge(vDV.h, v_yev(c, 1))
            sem = sYE[c // 2]
            for tb in range(4):
                rows = slice(c * ECH + tb * 128, c * ECH + (tb + 1) * 128)
                e.dma_start(out=YEIN.ap()[rows, :], in_=s_yout[:, tb * D:(tb + 1) * D]).then_inc(sem.h, 16); sem.n += 16
        assert sYE[c // 2].n == ye_ct(c)

    if PH <= 7:
        st0.close()
        return nc

    fin_dv = []
    out_sy = []
    for tb in range(16):
        pg = tb % 4
        xg = tb % 2
        @block.gpsimd
        def _(e, tb=tb, pg=pg):
            if tb == 0:
                e.wait_ge(cCC.h, m_cc_ye)
                e.wait_ge(vDV.h, m_own)
                e.wait_ge(sZ.h, m_sy_zero)
            ybound = e.to_reg(ZROW)
            if tb >= 4:
                e.wait_ge(vDV.h, fin_dv[tb - 4])
            sem = gFG[tb % 4]
            e.indirect_dma_start(out=s_ga0[:, pg * D:(pg + 1) * D], out_offset=None, in_=YEOUT.ap(),
                                 in_offset=IndirectOffsetOnAxis(ap=s_gidxOwnU[:, tb:tb + 1], axis=0),
                                 bounds_check=ybound, oob_is_err=False
                                 ).then_inc(sem.h, 16); sem.n += 16
            e.indirect_dma_start(out=s_ga1[:, pg * D:(pg + 1) * D], out_offset=None, in_=YEOUT.ap(),
                                 in_offset=IndirectOffsetOnAxis(ap=s_gidxOwnU[:, 16 + tb:17 + tb], axis=0),
                                 bounds_check=ybound, oob_is_err=False
                                 ).then_inc(sem.h, 16); sem.n += 16
        m_ga = gFG[tb % 4].n
        @block.sync
        def _(e, tb=tb, xg=xg):
            if tb < 2:
                e.wait_ge(tPE.h, t_y(NECH - 1, 1))
            else:
                e.wait_ge(vDV.h, fin_dv[tb - 2])
            e.dma_start(out=s_x1r[:, xg * D:(xg + 1) * D],
                        in_=X1D.ap()[tb * 128:(tb + 1) * 128, :]).then_inc(sXR.h, 16); sXR.n += 16
        m_x1r = sXR.n
        @block.vector
        def _(e, tb=tb, pg=pg, xg=xg, m_ga=m_ga, m_x1r=m_x1r):
            e.wait_ge(gFG[tb % 4].h, m_ga)
            e.wait_ge(sXR.h, m_x1r)
            if tb >= 1:
                e.wait_ge(sOUT.h, out_sy[tb - 1])
            def step(ins):
                ins.then_inc(vDV.h, 1); vDV.n += 1
                e.wait_ge(vDV.h, vDV.n)
            step(e.scalar_tensor_tensor(s_fb, s_ga0[:, pg * D:(pg + 1) * D],
                                        s_gOwn[:, tb:tb + 1], s_x1r[:, xg * D:(xg + 1) * D],
                                        op0=ALU.mult, op1=ALU.add))
            step(e.scalar_tensor_tensor(s_fin, s_ga1[:, pg * D:(pg + 1) * D],
                                        s_gOwn[:, 16 + tb:17 + tb], s_fb,
                                        op0=ALU.mult, op1=ALU.add))
            fin_dv.append(vDV.n)
        @block.sync
        def _(e, tb=tb):
            e.wait_ge(vDV.h, fin_dv[tb])
            e.dma_start(out=out[tb * 128:(tb + 1) * 128, :],
                        in_=s_fin).then_inc(sOUT.h, 16); sOUT.n += 16
            out_sy.append(sOUT.n)

    m_final = sOUT.n
    @block.sync
    def _(e):
        e.wait_ge(sOUT.h, m_final)

    st0.close()
    return nc


_CACHE = {}


def kernel(**inputs):
    x = np.asarray(inputs["L0_attn_x"], np.float32)
    Wq = np.asarray(inputs["L0_attn_Wq"], np.float32)
    bq = np.asarray(inputs["L0_attn_bq"], np.float32)
    Wk = np.asarray(inputs["L0_attn_Wk"], np.float32)
    bk = np.asarray(inputs["L0_attn_bk"], np.float32)
    Wv = np.asarray(inputs["L0_attn_Wv"], np.float32)
    bv = np.asarray(inputs["L0_attn_bv"], np.float32)
    Wo = np.asarray(inputs["L0_attn_Wo"], np.float32)
    bo = np.asarray(inputs["L0_attn_bo"], np.float32)
    Wg = np.asarray(inputs["L0_ffn_Wg"], np.float32)
    bg = np.asarray(inputs["L0_ffn_bg"], np.float32)
    W1 = np.asarray(inputs["L0_ffn_W1"], np.float32)
    b1 = np.asarray(inputs["L0_ffn_b1"], np.float32)
    W2 = np.asarray(inputs["L0_ffn_W2"], np.float32)
    b2 = np.asarray(inputs["L0_ffn_b2"], np.float32)

    ph = int(os.environ.get("KPH", "8"))
    sub = int(os.environ.get("KSUB", "9"))
    if _CACHE.get("ph") != (ph, sub):
        _CACHE["nc"] = build_nc(ph, sub)
        _CACHE["ph"] = (ph, sub)
    nc = _CACHE["nc"]

    xf = np.ascontiguousarray(x.reshape(NCORES, TC, D))
    wqT = np.ascontiguousarray(Wq.T); wkT = np.ascontiguousarray(Wk.T)
    wvT = np.ascontiguousarray(Wv.T); woT = np.ascontiguousarray(Wo.T)
    idn = np.eye(128, dtype=np.float32)
    su = np.triu(np.ones((128, 128), np.float32), 1)
    ones_r = np.ones((1, 128), np.float32)

    in_maps = []
    for c in range(NCORES):
        sel = np.zeros((256, 32), np.float32)
        for j in range(16):
            sel[c * 16 + j, j] = 1.0
            sel[128 + c * 16 + j, 16 + j] = 1.0
        in_maps.append(dict(
            x_sl=xf[c],
            wqT=wqT, wkT=wkT, wvT=wvT, woT=woT,
            bq_c=np.ascontiguousarray(bq.reshape(D, 1)),
            bk_r=np.ascontiguousarray(bk.reshape(1, D)),
            bv_r=np.ascontiguousarray(bv.reshape(1, D)),
            bo_r=np.ascontiguousarray(bo.reshape(1, D)),
            wg=Wg, bg_r=np.ascontiguousarray(bg.reshape(1, E)),
            w1c=np.ascontiguousarray(W1[c]),
            b1_c=np.ascontiguousarray(b1[c].reshape(FF, 1)),
            w2c=np.ascontiguousarray(W2[c]),
            b2_r=np.ascontiguousarray(b2[c].reshape(1, D)),
            idn=idn, su=su, ones_r=ones_r,
            selmat=sel,
            myexp=np.full((128, 1), float(c), np.float32),
        ))

    res = run_bass_kernel_spmd(nc, in_maps, core_ids=list(range(NCORES)))
    shards = [np.asarray(res.results[c]["out_sl"], np.float32) for c in range(NCORES)]
    return np.concatenate(shards, axis=0).reshape(4, 4096, D)
